# version 12
# baseline (speedup 1.0000x reference)
import os
import numpy as np
import ml_dtypes
import concourse.bass as bass
import concourse.mybir as mybir
import bass_rust
from concourse import library_config
from concourse.bass_utils import run_bass_kernel_spmd
from contextlib import ExitStack

F32 = mybir.dt.float32
BF16 = mybir.dt.bfloat16
OP = mybir.AluOpType
AF = mybir.ActivationFunctionType
AX = mybir.AxisListType.X
RED = bass_rust.ReduceOp

PI = np.pi
TWO_PI = 2.0 * np.pi
CLAMP = float(np.float32(3.1415925))
INV2PI = 1.0 / (2.0 * np.pi)
CW1 = 6.283203125
CW2 = float(2.0 * np.pi - 6.283203125)
I32 = mybir.dt.int32
MASK12 = -4096
MASK13 = -8192
MASK15 = -32768
LN_EPS = 1e-5
Q_EPS = 1e-5

D = 2048
F = 4096
FF = 5504
T = 2048
TG = 256
NTG = T // TG
ST = 512
NST = T // ST
NKD = D // 128
NFC = F // 128
NOB = FF // 128
NCORES = 8

ENGINES = ("sync", "scalar", "vector", "tensor", "gpsimd")
SIM_DRAINS = os.environ.get("BASS_SIM_FP64") == "1" or     os.environ.get("BASS_FORCE_DRAINS") == "1"


class Ctr:
    def __init__(self, handle, name):
        self.h = handle
        self.name = name
        self.n = 0

    def inc(self, k=1):
        self.n += k
        return self.n


def build_nc(replica_groups=None):
    if replica_groups is None:
        replica_groups = [list(range(NCORES))]
    nc = bass.Bass("TRN2", target_bir_lowering=False, debug=False)

    xt_d = nc.dram_tensor("xt", [D, T], F32, kind="ExternalInput").ap()
    x_d = nc.dram_tensor("x", [T, D], F32, kind="ExternalInput").ap()
    wkv_d = nc.dram_tensor("wkv", [D, F], F32, kind="ExternalInput").ap()
    bkv_d = nc.dram_tensor("bkv", [1, F], F32, kind="ExternalInput").ap()
    wg_d = nc.dram_tensor("wg", [FF, D], F32, kind="ExternalInput").ap()
    wu_d = nc.dram_tensor("wu", [FF, D], F32, kind="ExternalInput").ap()
    wd_d = nc.dram_tensor("wd", [D, FF], F32, kind="ExternalInput").ap()
    pos_d = nc.dram_tensor("pos", [1, T], F32, kind="ExternalInput").ap()
    jc_d = nc.dram_tensor("jc", [128, 4], F32, kind="ExternalInput").ap()
    eyeb_d = nc.dram_tensor("eyeb", [128, 128], BF16, kind="ExternalInput").ap()
    eyef_d = nc.dram_tensor("eyef", [128, 128], F32, kind="ExternalInput").ap()
    selp_d = nc.dram_tensor("selp", [128, 2], F32, kind="ExternalInput").ap()
    m32_d = nc.dram_tensor("m32", [64, 1], F32, kind="ExternalInput").ap()
    y_d = nc.dram_tensor("y", [T, D], F32, kind="ExternalOutput").ap()

    wbs_d = nc.dram_tensor("wbs", [NFC, D, 128], BF16).ap()
    wqgt_d = nc.dram_tensor("wqgt", [NOB, D, 128], BF16).ap()
    wqut_d = nc.dram_tensor("wqut", [NOB, D, 128], BF16).ap()
    wqdt_d = nc.dram_tensor("wqdt", [FF, D], BF16).ap()
    arin_d = nc.dram_tensor("arin", [64, 128], F32).ap()
    arout_d = nc.dram_tensor("arout", [64, 128], F32).ap()
    rowf_d = nc.dram_tensor("rowf", [4, 128], F32).ap()
    prsb_d = nc.dram_tensor("prsb", [16, 128], BF16).ap()
    csb_d = nc.dram_tensor("csb", [32, 128], F32).ap()

    with ExitStack() as CTX:
        SEM = lambda n: Ctr(CTX.enter_context(nc.semaphore(n)), n)
        q_tiny = SEM("q_tiny")
        q_xt = [SEM("q_xt0"), SEM("q_xt1")]
        q_wl = [SEM("q_wl0"), SEM("q_wl1")]
        q_wf = [SEM("q_wf0"), SEM("q_wf1")]
        q_x = SEM("q_x")
        q_wbs = SEM("q_wbs")
        q_st = [SEM("q_st0"), SEM("q_st1")]
        q_ar = SEM("q_ar")
        q_y = [SEM("q_y0"), SEM("q_y1")]
        qc = SEM("qc")
        q_u2 = SEM("q_u2")
        SYNC_QS = [q_tiny, q_xt[0], q_xt[1], q_wl[0], q_wl[1], q_x]
        GP_QS = [q_wbs, q_st[0], q_st[1], q_ar, q_y[0], q_y[1], q_wf[1]]
        SC_QS = [qc, q_u2, q_wf[0]]
        sv = SEM("sv")
        ss = SEM("ss")
        stn = SEM("stn")
        sg = SEM("sg")
        scc = SEM("scc")
        bar = SEM("bar")

        prog = {e: [] for e in ENGINES}

        def mk(engine, fn, waits=(), inc=None, k=1, drain=False):
            for (c, v) in waits:
                assert v <= c.n, f"[{engine}] wait {c.name}>={v} but built {c.n}"
            tgt = inc.inc(k) if inc is not None else None

            def go(E, fn=fn, waits=tuple(waits), inc=inc, k=k, drain=drain):
                for (c, v) in waits:
                    if v > 0:
                        E.wait_ge(c.h, v)
                r = fn(E)
                if inc is not None:
                    r.then_inc(inc.h, k)
                if drain:
                    E.drain()
            prog[engine].append(go)
            return tgt

        V = lambda fn, waits=(), inc=sv, k=1, dr=True: mk("vector", fn, waits, inc, k, drain=dr)
        Sc = lambda fn, waits=(), inc=ss, k=1, dr=True: mk("scalar", fn, waits, inc, k, drain=dr)
        Tn = lambda fn, waits=(), inc=None, k=1: mk("tensor", fn, waits, inc, k)
        G = lambda fn, waits=(), inc=sg, k=1: mk("gpsimd", fn, waits, inc, k)

        def dma(qeng, qctr, out, in_, waits=()):
            return mk(qeng, lambda E, o=out, i=in_: E.dma_start(out=o, in_=i),
                      waits, qctr, 16)

        def drain_nop(E):
            E.drain()
            return E.nop()

        def barrier():
            base = bar.n
            mk("sync", lambda E: E.nop(),
               waits=[(q, q.n) for q in SYNC_QS], inc=bar)
            mk("scalar", drain_nop, waits=[(q, q.n) for q in SC_QS], inc=bar)
            mk("vector", drain_nop, waits=(), inc=bar)
            mk("tensor", lambda E: E.nop(), waits=(), inc=bar)
            mk("gpsimd", lambda E: E.nop(),
               waits=[(q, q.n) for q in GP_QS], inc=bar)
            tgt = base + 5
            for e in ENGINES:
                mk(e, lambda E: E.nop(), waits=[(bar, tgt)])
            return tgt

        def flush_prog(block):
            for e in ENGINES:
                ops = tuple(prog[e])
                prog[e] = []
                if not ops:
                    continue

                def body(E, ops=ops):
                    for fn in ops:
                        fn(E)
                getattr(block, e)(body)

        SBP = lambda n, s, dt=F32: CTX.enter_context(nc.sbuf_tensor(n, s, dt))
        s_invf = SBP("s_invf", [128, 4])
        s_jc = SBP("s_jc", [128, 4])
        s_eyeb = SBP("s_eyeb", [128, 128], BF16)
        s_eyef = SBP("s_eyef", [128, 128])
        s_onesb = SBP("s_onesb", [128, 1], BF16)
        s_ones_row_b = SBP("s_ones_row_b", [1, 128], BF16)
        s_onesf = SBP("s_onesf", [1, 128])
        s_gA = SBP("s_gA", [128, D])
        s_A = SBP("s_A", [128, 16])
        s_acol = SBP("s_acol", [128, 2])
        s_rsg = SBP("s_rsg", [128, NOB])
        s_rsu = SBP("s_rsu", [128, NOB])
        s_rsd = SBP("s_rsd", [128, 16])
        s_nrsg = SBP("s_nrsg", [128, NOB])
        s_nrsu = SBP("s_nrsu", [128, NOB])
        s_prsdf = SBP("s_prsdf", [16, 128])
        s_prsdb = SBP("s_prsdb", [16, 128], BF16)
        s_rs1 = SBP("s_rs1", [128, 1])
        s_rsacc = SBP("s_rsacc", [128, 1])

        with nc.Block() as block:
            with ExitStack() as PA:
                SB = lambda n, s, dt=F32: PA.enter_context(nc.sbuf_tensor(n, s, dt))
                PS = lambda n, s: PA.enter_context(nc.psum_tensor(n, s, F32))

                s_wld = [SB(f"s_wld{i}", [128, D]) for i in range(2)]
                s_wbf = SB("s_wbf", [128, F], BF16)
                s_q1 = SB("s_q1", [128, D], BF16)
                s_q2 = SB("s_q2", [128, D], BF16)
                s_wqo = SB("s_wqo", [128, D], BF16)
                s_red = SB("s_red", [128, 16])
                s_scale = SB("s_scale", [128, 16])
                s_nscale = SB("s_nscale", [128, 16])
                s_half = SB("s_half", [128, 16])
                s_nhalf = SB("s_nhalf", [128, 16])
                s_stripe = [SB(f"s_stripe{i}", [128, D], BF16) for i in range(2)]
                s_posb = SB("s_posb", [128, T])
                s_brow = SB("s_brow", [1, F])
                s_posr = SB("s_posr", [1, T])
                s_r0 = SB("s_r0", [1, D])
                s_r1 = SB("s_r1", [1, D])
                s_csw = SB("s_csw", [1, F])
                s_cswT = SB("s_cswT", [32, 128])
                s_cswc = SB("s_cswc", [128, 32])
                s_ar2 = SB("s_ar2", [64, 128])
                s_arin = SB("s_arin", [64, 128])
                s_m32 = SB("s_m32", [64, 1])
                s_selp = SB("s_selp", [128, 2])
                s_xtld = [SB(f"s_xtld{i}", [128, TG]) for i in range(2)]
                s_xtb = SB("s_xtb", [128, NKD * TG], BF16)
                s_sq = SB("s_sq", [128, TG], BF16)
                s_kvt = SB("s_kvt", [128, NFC * TG])
                s_wf = [SB(f"s_wf{i}", [128, NKD * 128], BF16) for i in range(2)]
                TG2 = 2 * TG
                s_tabc = SB("s_tabc", [128, TG2])
                s_tabs = SB("s_tabs", [128, TG2])
                s_frq = SB("s_frq", [128, TG2])
                s_u1 = SB("s_u1", [128, TG2])
                s_u2 = SB("s_u2", [128, TG2])
                s_u3 = SB("s_u3", [128, TG2])
                s_u4 = SB("s_u4", [128, TG2])
                s_tmA = SB("s_tmA", [128, TG2])
                s_tmB = SB("s_tmB", [128, TG2])
                s_tmC = SB("s_tmC", [128, TG2])
                s_tmD = SB("s_tmD", [128, TG2])
                s_jk = SB("s_jk", [128, TG])
                s_mrow = SB("s_mrow", [1, TG])
                s_nmrow = SB("s_nmrow", [1, TG])
                s_varow = SB("s_varow", [1, TG])
                s_sdrow = SB("s_sdrow", [1, TG])
                s_rstdrow = SB("s_rstdrow", [1, TG])
                s_rbc2 = SB("s_rbc2", [128, 2 * TG])

                p_kv = [PS(f"p_kv{i}", [128, 512]) for i in range(4)]
                p_row = PS("p_row", [1, 512])
                p_tr = [PA.enter_context(nc.psum_tensor(f"p_tr{i}", [128, 1024], BF16))
                        for i in range(2)]
                p_A2 = PS("p_A2", [128, 512])
                p_A = p_A2

                for _dst, _src in [(s_jc[:], jc_d), (s_eyeb[:], eyeb_d),
                                   (s_eyef[:], eyef_d), (s_brow[:], bkv_d),
                                   (s_selp[:], selp_d), (s_m32[:], m32_d),
                                   (s_posr[:], pos_d)]:
                    dma("sync", q_tiny, _dst, _src)
                T_TINY = q_tiny.n

                V(lambda E: E.memset(s_onesb[:], 1.0))
                V(lambda E: E.memset(s_ones_row_b[:], 1.0))
                v_onef = V(lambda E: E.memset(s_onesf[:], 1.0))
                V(lambda E: E.memset(s_A[:], 0.0))
                v_pb = 0
                for j in range(4):
                    wpb = [(q_tiny, T_TINY), (sv, v_onef)]
                    if v_pb:
                        wpb.append((sv, v_pb))
                    t = Tn(lambda E, j=j: E.matmul(
                        p_kv[1][:, 0:512], s_onesf[:],
                        s_posr[0:1, 512 * j:512 * (j + 1)],
                        start=True, stop=True), waits=wpb, inc=stn)
                    v_pb = V(lambda E, j=j: E.tensor_copy(
                        s_posb[:, 512 * j:512 * (j + 1)], p_kv[1][:, 0:512]),
                        waits=[(stn, t)])
                g_posb = 0
                cexp = -np.log(10000.0) / 512.0
                v_jc = V(lambda E: E.tensor_scalar(s_invf[:], s_jc[:], cexp, None, OP.mult),
                         waits=[(q_tiny, T_TINY)])
                s_if = Sc(lambda E: E.activation(s_invf[:], s_invf[:], AF.Exp),
                          waits=[(sv, v_jc)])

                wld_free = [0, 0]
                last_wbs_store = 0
                csw_mm = 0
                for k in range(NKD):
                    tw0 = dma("sync", q_wl[0], s_wld[0][:],
                              wkv_d[128 * k:128 * (k + 1), 0:D],
                              waits=[(sv, wld_free[0])])
                    tw1 = dma("sync", q_wl[1], s_wld[1][:],
                              wkv_d[128 * k:128 * (k + 1), D:F],
                              waits=[(sv, wld_free[1])])
                    st_wait = [(q_wl[0], tw0)]
                    if last_wbs_store:
                        st_wait.append((q_wbs, last_wbs_store))
                    if csw_mm:
                        st_wait.append((stn, csw_mm))
                    v0 = V(lambda E: E.tensor_copy(s_wbf[:, 0:D], s_wld[0][:]),
                           waits=st_wait)
                    v1 = V(lambda E: E.tensor_copy(s_wbf[:, D:F], s_wld[1][:]),
                           waits=[(q_wl[1], tw1)])
                    wld_free = [v0, v1]
                    ap = wbs_d.rearrange("f d j -> d f j")[128 * k:128 * (k + 1), :, :]
                    last_wbs_store = dma(
                        "gpsimd", q_wbs, ap,
                        s_wbf[:].rearrange("d (f j) -> d f j", j=128),
                        waits=[(sv, v1)])
                    for fc in range(NFC):
                        csw_mm = Tn(lambda E, fc=fc, k=k: E.matmul(
                            p_A2[:, fc:fc + 1],
                            s_wbf[:, 128 * fc:128 * (fc + 1)], s_onesb[:],
                            start=(k == 0 and fc == 0),
                            stop=(k == NKD - 1 and fc == NFC - 1)),
                            waits=[(sv, v1)] if fc == 0 else [],
                            inc=stn if (fc == NFC - 1) else None)
                v_c1 = V(lambda E: E.tensor_copy(s_cswc[:], p_A2[:, 0:32]),
                         waits=[(stn, csw_mm)])
                t_cswT = Tn(lambda E: E.transpose(
                    p_kv[0][0:32, 0:128], s_cswc[:], s_eyef[:]),
                    waits=[(sv, v_c1), (q_tiny, T_TINY)], inc=stn)
                v_c2 = V(lambda E: E.tensor_copy(s_cswT[:], p_kv[0][0:32, 0:128]),
                         waits=[(stn, t_cswT)])
                t_cb1 = dma("gpsimd", q_ar, csb_d, s_cswT[:], waits=[(sv, v_c2)])
                t_csw = dma("gpsimd", q_ar, s_csw[:],
                            csb_d.rearrange("a b -> (a b)").rearrange(
                                "(o n) -> o n", o=1), waits=[(q_ar, t_cb1)])

                qtiles = []
                for ob in range(NOB):
                    qtiles.append(("g", ob, 0, D))
                    qtiles.append(("u", ob, 0, D))
                for ob in range(16):
                    qtiles.append(("d", ob, 0, D))
                    qtiles.append(("d", ob, D, D))
                    qtiles.append(("d", ob, 2 * D, FF - 2 * D))
                qsrc = {"g": wg_d, "u": wu_d, "d": wd_d}
                qdst = {"g": wqgt_d, "u": wqut_d, "d": wqdt_d}
                qst = {"wld": list(wld_free), "stripe": [0, 0], "pt": [0, 0],
                       "i": 0, "wqo_t": 0}

                def emit_qtile():
                    i = qst["i"]
                    if i >= len(qtiles):
                        return
                    qst["i"] += 1
                    mat, ob, i0, wid = qtiles[i]
                    G_ = wid // 128
                    buf = i % 2
                    tL = dma("sync", q_wl[buf], s_wld[buf][:, 0:wid],
                             qsrc[mat][128 * ob:128 * (ob + 1), i0:i0 + wid],
                             waits=[(sv, qst["wld"][buf])])
                    V(lambda E, buf=buf, wid=wid, G_=G_: E.reduce_sum(
                        s_red[:, 0:G_],
                        s_wld[buf][:, 0:wid].rearrange("p (g k) -> p g k", k=128),
                        axis=AX, apply_absolute_value=True),
                        waits=[(q_wl[buf], tL)])
                    V(lambda E, G_=G_: E.tensor_scalar(
                        s_scale[:, 0:G_], s_red[:, 0:G_], 1.0 / 128.0, Q_EPS,
                        OP.mult, OP.add))
                    V(lambda E, G_=G_: E.tensor_scalar(
                        s_half[:, 0:G_], s_scale[:, 0:G_], 0.5, None, OP.mult),
                        dr=False)
                    V(lambda E, G_=G_: E.tensor_scalar(
                        s_nhalf[:, 0:G_], s_scale[:, 0:G_], -0.5, None, OP.mult),
                        dr=False)
                    V(lambda E, G_=G_: E.tensor_scalar(
                        s_nscale[:, 0:G_], s_scale[:, 0:G_], -1.0, None, OP.mult))
                    v = 0
                    for g in range(G_):
                        sl = slice(128 * g, 128 * (g + 1))
                        V(lambda E, buf=buf, g=g, sl=sl: E.tensor_scalar(
                            s_q1[:, sl], s_wld[buf][:, sl], s_half[:, g:g + 1],
                            s_scale[:, g:g + 1], OP.is_gt, OP.mult), dr=False)
                        v = V(lambda E, buf=buf, g=g, sl=sl: E.tensor_scalar(
                            s_q2[:, sl], s_wld[buf][:, sl], s_nhalf[:, g:g + 1],
                            s_nscale[:, g:g + 1], OP.is_lt, OP.mult),
                            dr=(g == G_ - 1))
                    qst["wld"][buf] = v
                    v = V(lambda E, wid=wid: E.scalar_tensor_tensor(
                        s_wqo[:, 0:wid], s_q1[:, 0:wid], 1.0, s_q2[:, 0:wid],
                        OP.mult, OP.add, accum_out=s_rs1[:]),
                        waits=[(stn, qst["wqo_t"])] if qst["wqo_t"] else [])
                    if mat == "d":
                        if i0 == 0:
                            v = V(lambda E: E.tensor_copy(s_rsacc[:], s_rs1[:]))
                        else:
                            v = V(lambda E: E.tensor_tensor(
                                s_rsacc[:], s_rsacc[:], s_rs1[:], OP.add))
                        if i0 == 2 * D:
                            v = V(lambda E, ob=ob: E.tensor_copy(
                                s_rsd[:, ob:ob + 1], s_rsacc[:]))
                    else:
                        dst = s_rsg if mat == "g" else s_rsu
                        v = V(lambda E, dst=dst, ob=ob: E.tensor_copy(
                            dst[:, ob:ob + 1], s_rs1[:]))
                    vq = v
                    nbank = (G_ + 3) // 4
                    sev = 0
                    sb_ = i % 2
                    for nb in range(nbank):
                        bank = nb % 2
                        nblk = min(4, G_ - 4 * nb)
                        wtx = [(sv, vq), (q_tiny, T_TINY)] if nb == 0 else []
                        if qst["pt"][bank]:
                            wtx = wtx + [(ss, qst["pt"][bank])]
                        t = 0
                        for j in range(nblk):
                            g = 4 * nb + j
                            t = Tn(lambda E, bank=bank, j=j, g=g: E.transpose(
                                p_tr[bank][:, 128 * j:128 * (j + 1)],
                                s_wqo[:, 128 * g:128 * (g + 1)], s_eyeb[:]),
                                waits=wtx if j == 0 else [],
                                inc=stn if j == nblk - 1 else None)
                        wse = [(stn, t)]
                        if nb == 0 and qst["stripe"][sb_]:
                            wse.append((q_st[sb_], qst["stripe"][sb_]))
                        sev = Sc(lambda E, bank=bank, nb=nb, nblk=nblk, sb_=sb_: E.copy(
                            s_stripe[sb_][:, 512 * nb:512 * nb + 128 * nblk],
                            p_tr[bank][:, 0:128 * nblk]), waits=wse)
                        qst["pt"][bank] = sev
                        qst["wqo_t"] = t
                    if mat == "d":
                        ap = qdst[mat][i0:i0 + wid,
                                       128 * ob:128 * (ob + 1)].rearrange(
                            "(g k) j -> k g j", k=128)
                        src = s_stripe[sb_][:, 0:wid].rearrange(
                            "k (g j) -> k g j", j=128)
                    else:
                        ap = qdst[mat][ob]
                        src = s_stripe[sb_][:, 0:wid]
                    qst["stripe"][sb_] = dma(
                        "gpsimd", q_st[sb_], ap, src, waits=[(ss, sev)])

                kv_evac = {}
                bank_free = [v_c2, v_pb, 0, 0]
                wf_free = [0, 0]
                prod_last = 0
                xtb_free_t = 0
                prow_free_v = 0
                rbc_prev = 0
                tab_prev = 0

                for tg in range(NTG):
                    c0 = TG * tg
                    v_cvt = 0
                    tprev = [0, 0]
                    for k in range(NKD):
                        lb = k % 2
                        wxt = [(sv, tprev[lb])] if tprev[lb] else []
                        tX = dma("sync", q_xt[lb], s_xtld[lb][:],
                                 xt_d[128 * k:128 * (k + 1), c0:c0 + TG], waits=wxt)
                        wv = [(q_xt[lb], tX)]
                        if k == 0 and xtb_free_t:
                            wv.append((stn, xtb_free_t))
                        v_cvt = V(lambda E, k=k, lb=lb: E.tensor_copy(
                            s_xtb[:, TG * k:TG * (k + 1)], s_xtld[lb][:]), waits=wv)
                        tprev[lb] = v_cvt
                    wsx = [(sv, v_cvt), (sv, prow_free_v)]
                    t_sx = 0
                    for k in range(NKD):
                        t_sx = Tn(lambda E, k=k: E.matmul(
                            p_row[0:1, 0:TG], s_onesb[:],
                            s_xtb[:, TG * k:TG * (k + 1)],
                            start=(k == 0), stop=(k == NKD - 1)),
                            waits=wsx if k == 0 else [],
                            inc=stn if k == NKD - 1 else None)
                    prev_mm = 0
                    for k in range(NKD):
                        wsq = [(sv, v_cvt)] if k == 0 else [(stn, prev_mm)]
                        s = Sc(lambda E, k=k: E.activation(
                            s_sq[:], s_xtb[:, TG * k:TG * (k + 1)], AF.Square),
                            waits=wsq)
                        prev_mm = Tn(lambda E, k=k: E.matmul(
                            p_row[0:1, 256:256 + TG], s_onesb[:], s_sq[:],
                            start=(k == 0), stop=(k == NKD - 1)),
                            waits=[(ss, s)], inc=stn)
                    t_sxx = prev_mm
                    V(lambda E: E.tensor_scalar(
                        s_mrow[:], p_row[0:1, 0:TG], 1.0 / D, None, OP.mult),
                        waits=[(stn, t_sx), (stn, t_sxx)])
                    V(lambda E: E.scalar_tensor_tensor(
                        s_varow[:], s_mrow[:], -1.0, s_mrow[:], OP.mult, OP.mult))
                    V(lambda E: E.scalar_tensor_tensor(
                        s_varow[:], p_row[0:1, 256:256 + TG], 1.0 / D, s_varow[:],
                        OP.mult, OP.add))
                    v_ve = V(lambda E: E.tensor_scalar(
                        s_varow[:], s_varow[:], LN_EPS, None, OP.add))
                    prow_free_v = v_ve
                    s = Sc(lambda E: E.activation(s_sdrow[:], s_varow[:], AF.Sqrt),
                           waits=[(sv, v_ve)])
                    V(lambda E: E.reciprocal(s_rstdrow[:], s_sdrow[:]),
                      waits=[(ss, s)])
                    V(lambda E: E.tensor_scalar(
                        s_mrow[:], s_mrow[:], -1.0, None, OP.mult))
                    v_rows = V(lambda E: E.tensor_copy(s_nmrow[:], s_mrow[:]))
                    wrb = [(sv, v_rows), (sv, v_c2)]
                    if rbc_prev:
                        wrb.append((sv, rbc_prev))
                    t_rb = Tn(lambda E: E.matmul(
                        p_A2[:, 256:256 + TG], s_onesf[:], s_rstdrow[:],
                        start=True, stop=True), waits=wrb, inc=stn)
                    V(lambda E: E.tensor_copy(
                        s_rbc2[:, 0:TG], p_A2[:, 256:256 + TG]),
                        waits=[(stn, t_rb)], dr=False)
                    v_rb = V(lambda E: E.tensor_copy(
                        s_rbc2[:, TG:2 * TG], p_A2[:, 256:256 + TG]))
                    rbc_prev = v_rb

                    for fc in range(NFC):
                        wbuf = fc % 2
                        pr, half = fc // 2, fc % 2
                        bank = pr % 4
                        wws = ([(stn, wf_free[wbuf])] if wf_free[wbuf]
                               else [(q_wbs, last_wbs_store)])
                        tW = dma("scalar" if wbuf == 0 else "gpsimd", q_wf[wbuf],
                                 s_wf[wbuf][:].rearrange("p (k j) -> p k j", j=128),
                                 wbs_d[fc, :, :].rearrange("(k p) j -> p k j", p=128),
                                 waits=wws)
                        bw = [(q_wf[wbuf], tW), (sv, v_cvt), (q_ar, t_csw),
                              (sv, v_rows)]
                        if half == 0 and bank_free[bank]:
                            bw.append((sv, bank_free[bank]))
                        ps = p_kv[bank][:, TG * half:TG * (half + 1)]
                        for k in range(NKD):
                            Tn(lambda E, k=k, wbuf=wbuf, ps=ps, half=half:
                               E.matmul(
                                   ps, s_wf[wbuf][:, 128 * k:128 * (k + 1)],
                                   s_xtb[:, TG * k:TG * (k + 1)],
                                   start=(k == 0 and half == 0), stop=False),
                               waits=bw if k == 0 else [])
                        Tn(lambda E, fc=fc, ps=ps: E.matmul(
                            ps, s_csw[0:1, 128 * fc:128 * (fc + 1)],
                            s_nmrow[:], start=False, stop=False))
                        t_f = Tn(lambda E, fc=fc, ps=ps, half=half: E.matmul(
                            ps, s_brow[0:1, 128 * fc:128 * (fc + 1)],
                            s_sdrow[:], start=False, stop=(half == 1)), inc=stn)
                        wf_free[wbuf] = t_f
                        if half == 1:
                            v = V(lambda E, pr=pr, bank=bank: E.tensor_tensor(
                                s_kvt[:, TG2 * pr:TG2 * (pr + 1)],
                                p_kv[bank][:, 0:TG2], s_rbc2[:], OP.mult),
                                waits=[(stn, t_f), (sv, v_rb)])
                            bank_free[bank] = v
                            kv_evac[pr] = v
                    xtb_free_t = max(wf_free)

                    for c in range(4):
                        V(lambda E, c=c, c0=c0: E.tensor_scalar(
                            s_frq[:, 0:TG], s_posb[:, c0:c0 + TG],
                            s_invf[:, c:c + 1], None, OP.mult),
                            waits=[(sv, v_pb), (ss, s_if)], dr=False)
                        V(lambda E, c=c, c0=c0: E.tensor_scalar(
                            s_frq[:, TG:TG2], s_posb[:, c0:c0 + TG],
                            s_invf[:, c:c + 1], None, OP.mult))
                        def wrap_pair(srcs, qshs, pshs, un, rn, wait0):
                            V(lambda E: E.tensor_scalar(
                                un[0][:], srcs[0](), INV2PI, 2048.5 + qshs[0],
                                OP.mult, OP.add), waits=wait0, dr=False)
                            V(lambda E: E.tensor_scalar(
                                un[1][:], srcs[1](), INV2PI, 2048.5 + qshs[1],
                                OP.mult, OP.add))
                            for j in (0, 1):
                                V(lambda E, j=j: E.tensor_scalar(
                                    un[j][:].bitcast(I32), un[j][:].bitcast(I32),
                                    MASK12, None, OP.bitwise_and), dr=(j == 1))
                            for j in (0, 1):
                                V(lambda E, j=j: E.tensor_scalar(
                                    un[j][:], un[j][:], -2048.0, None, OP.add),
                                    dr=(j == 1))
                            for j in (0, 1):
                                V(lambda E, j=j: E.scalar_tensor_tensor(
                                    rn[j][:], un[j][:], -CW1, srcs[j](),
                                    OP.mult, OP.add), dr=(j == 1))
                            for j in (0, 1):
                                V(lambda E, j=j: E.scalar_tensor_tensor(
                                    rn[j][:], un[j][:], -CW2, rn[j][:],
                                    OP.mult, OP.add), dr=(j == 1))
                            vlast = 0
                            if SIM_DRAINS:
                                for j in (0, 1):
                                    V(lambda E, j=j: E.tensor_scalar(
                                        rn[j][:], rn[j][:], pshs[j], CLAMP,
                                        OP.add, OP.min), dr=(j == 1))
                                for j in (0, 1):
                                    vlast = V(lambda E, j=j: E.tensor_scalar(
                                        rn[j][:], rn[j][:], -CLAMP, None,
                                        OP.max), dr=(j == 1))
                            else:
                                for j in (0, 1):
                                    vlast = V(lambda E, j=j: E.tensor_scalar(
                                        rn[j][:], rn[j][:], pshs[j], None,
                                        OP.add), dr=(j == 1))
                            return vlast

                        v2_ = wrap_pair(
                            (lambda: s_frq[:], lambda: s_frq[:]),
                            (0.25, 0.0), (PI / 2, 0.0),
                            (s_u1, s_u3), (s_u2, s_u4),
                            [(ss, tab_prev)] if tab_prev else ())
                        Sc(lambda E: E.activation(s_tabc[:], s_u2[:], AF.Sin),
                           waits=[(sv, v2_)], dr=False)
                        s_tab = Sc(lambda E: E.activation(
                            s_tabs[:], s_u4[:], AF.Sin))
                        tab_prev = s_tab
                        pE, pO = c, c + 4
                        we = [(ss, s_tab), (sv, kv_evac[pE]), (sv, kv_evac[pO])]
                        V(lambda E, pE=pE: E.tensor_tensor(
                            s_tmA[:], s_kvt[:, TG2 * pE:TG2 * (pE + 1)],
                            s_tabc[:], OP.mult), waits=we, dr=False)
                        V(lambda E, pO=pO: E.tensor_tensor(
                            s_tmB[:], s_kvt[:, TG2 * pO:TG2 * (pO + 1)],
                            s_tabs[:], OP.mult), dr=False)
                        V(lambda E, pO=pO: E.tensor_tensor(
                            s_tmC[:], s_kvt[:, TG2 * pO:TG2 * (pO + 1)],
                            s_tabc[:], OP.mult), dr=False)
                        V(lambda E, pE=pE: E.tensor_tensor(
                            s_tmD[:], s_kvt[:, TG2 * pE:TG2 * (pE + 1)],
                            s_tabs[:], OP.mult))
                        v = V(lambda E, pE=pE: E.tensor_tensor(
                            s_kvt[:, TG2 * pE:TG2 * (pE + 1)], s_tmA[:], s_tmB[:],
                            OP.subtract), dr=False)
                        v = V(lambda E, pO=pO: E.tensor_tensor(
                            s_kvt[:, TG2 * pO:TG2 * (pO + 1)], s_tmC[:],
                            s_tmD[:], OP.add))
                        kv_evac[pE] = v
                        kv_evac[pO] = v
                    cos_prev = 0
                    for cp in range(8):
                        cE, cO = 2 * cp, 2 * cp + 1
                        wk = [(sv, kv_evac[cE]), (sv, kv_evac[cO])]
                        if cos_prev:
                            wk.append((ss, cos_prev))
                        V(lambda E, c=cE: E.tensor_scalar(
                            s_u1[:], s_kvt[:, TG2 * c:TG2 * (c + 1)], INV2PI,
                            1536.75, OP.mult, OP.add), waits=wk, dr=False)
                        V(lambda E, c=cO: E.tensor_scalar(
                            s_u3[:], s_kvt[:, TG2 * c:TG2 * (c + 1)], INV2PI,
                            1536.75, OP.mult, OP.add))
                        for u in (s_u1, s_u3):
                            V(lambda E, u=u: E.tensor_scalar(
                                u[:].bitcast(I32), u[:].bitcast(I32),
                                MASK13, None, OP.bitwise_and), dr=(u is s_u3))
                        for u in (s_u1, s_u3):
                            V(lambda E, u=u: E.tensor_scalar(
                                u[:], u[:], -1536.0, None, OP.add),
                                dr=(u is s_u3))
                        for (u, r, c) in ((s_u1, s_u2, cE), (s_u3, s_u4, cO)):
                            V(lambda E, u=u, r=r, c=c: E.scalar_tensor_tensor(
                                r[:], u[:], -CW1,
                                s_kvt[:, TG2 * c:TG2 * (c + 1)], OP.mult, OP.add),
                                dr=(u is s_u3))
                        for (u, r) in ((s_u1, s_u2), (s_u3, s_u4)):
                            V(lambda E, u=u, r=r: E.scalar_tensor_tensor(
                                r[:], u[:], -CW2, r[:], OP.mult, OP.add),
                                dr=(u is s_u3))
                        if SIM_DRAINS:
                            for r in (s_u2, s_u4):
                                V(lambda E, r=r: E.tensor_scalar(
                                    r[:], r[:], PI / 2, CLAMP, OP.add, OP.min),
                                    dr=(r is s_u4))
                            for r in (s_u2, s_u4):
                                v2_ = V(lambda E, r=r: E.tensor_scalar(
                                    r[:], r[:], -CLAMP, None, OP.max),
                                    dr=(r is s_u4))
                        else:
                            for r in (s_u2, s_u4):
                                v2_ = V(lambda E, r=r: E.tensor_scalar(
                                    r[:], r[:], PI / 2, None, OP.add),
                                    dr=(r is s_u4))
                        Sc(lambda E: E.activation(s_tmA[:], s_u2[:], AF.Sin),
                           waits=[(sv, v2_)], dr=False)
                        cos_prev = Sc(lambda E: E.activation(
                            s_tmB[:], s_u4[:], AF.Sin))
                        V(lambda E: E.scalar_tensor_tensor(
                            s_jk[:], s_tmA[:, 0:TG], 1.0, s_tmA[:, TG:TG2],
                            OP.mult, OP.mult, accum_out=s_acol[:, 0:1]),
                            waits=[(ss, cos_prev)], dr=False)
                        V(lambda E: E.scalar_tensor_tensor(
                            s_frq[:, 0:TG], s_tmB[:, 0:TG], 1.0,
                            s_tmB[:, TG:TG2], OP.mult, OP.mult,
                            accum_out=s_acol[:, 1:2]))
                        V(lambda E, c=cE: E.tensor_tensor(
                            s_A[:, c:c + 1], s_A[:, c:c + 1], s_acol[:, 0:1],
                            OP.add), dr=False)
                        prod_last = V(lambda E, c=cO: E.tensor_tensor(
                            s_A[:, c:c + 1], s_A[:, c:c + 1], s_acol[:, 1:2],
                            OP.add))
                    for _ in range(17):
                        emit_qtile()
                    barrier()
                while qst["i"] < len(qtiles):
                    emit_qtile()

                tA = Tn(lambda E: E.transpose(p_A[0:16, 0:128], s_A[:, 0:16],
                                              s_eyef[:]),
                        waits=[(sv, prod_last), (q_tiny, T_TINY)], inc=stn)
                V(lambda E: E.memset(s_ar2[:], 0.0),
                  waits=[(stn, tA), (q_tiny, T_TINY)])
                v_a0 = V(lambda E: E.tensor_copy(s_ar2[0:16, :], p_A[0:16, 0:128]))
                t_a1 = dma("gpsimd", q_ar, s_ar2[32:48, :], s_ar2[0:16, :],
                           waits=[(sv, v_a0)])
                v_arin = V(lambda E: E.tensor_scalar(
                    s_arin[:], s_ar2[:], s_m32[:, 0:1], None, OP.mult),
                    waits=[(q_ar, t_a1)])
                tAR1 = dma("gpsimd", q_ar, arin_d, s_arin[:], waits=[(sv, v_arin)])
                cc_t = mk("gpsimd", lambda E: E.collective_compute(
                    "AllReduce", OP.add, replica_groups=replica_groups,
                    ins=[arin_d.opt()], outs=[arout_d.opt()]),
                    waits=[(q_ar, tAR1)], inc=scc)
                tAR2 = dma("gpsimd", q_ar, s_r0[:],
                           arout_d.rearrange("a b -> (a b)").rearrange("(o n) -> o n", o=1)[:, 0:D],
                           waits=[(scc, cc_t)])
                tAR3 = dma("gpsimd", q_ar, s_r1[:],
                           arout_d.rearrange("a b -> (a b)").rearrange("(o n) -> o n", o=1)[:, 2 * D:3 * D])
                v_sel = v_arin
                for (dst, srow, off) in ((s_posb, s_r0, 0), (s_kvt, s_r1, 0)):
                    for j in range(4):
                        t = Tn(lambda E, srow=srow, j=j: E.matmul(
                            p_A2[:, 0:512], s_onesf[:],
                            srow[0:1, 512 * j:512 * (j + 1)],
                            start=True, stop=True),
                            waits=[(q_ar, tAR3), (sv, v_sel)], inc=stn)
                        v_sel = V(lambda E, dst=dst, j=j: E.tensor_copy(
                            dst[:, 512 * j:512 * (j + 1)], p_A2[:, 0:512]),
                            waits=[(stn, t)])
                V(lambda E: E.tensor_scalar(
                    s_kvt[:, D:2 * D], s_posb[:], s_selp[:, 0:1], None, OP.mult),
                    waits=[(sv, v_sel), (q_tiny, T_TINY)])
                V(lambda E: E.scalar_tensor_tensor(
                    s_kvt[:, D:2 * D], s_kvt[:, 0:D], s_selp[:, 1:2],
                    s_kvt[:, D:2 * D], OP.mult, OP.add))
                V(lambda E: E.tensor_scalar(
                    s_kvt[:, D:2 * D], s_kvt[:, D:2 * D], 0.5, 0.5,
                    OP.mult, OP.add))
                V(lambda E: E.tensor_copy(s_gA[:, 0:1024:2],
                                          s_kvt[:, D:D + 512]))
                V(lambda E: E.tensor_copy(s_gA[:, 1:1024:2],
                                          s_kvt[:, D + 512:D + 1024]))
                V(lambda E: E.tensor_copy(s_gA[:, 1024:2048],
                                          s_kvt[:, D + 1024:D + 2048]))

                V(lambda E: E.tensor_scalar(s_nrsg[:], s_rsg[:], -128.0, None, OP.mult))
                v_nrs = V(lambda E: E.tensor_scalar(
                    s_nrsu[:], s_rsu[:], -128.0, None, OP.mult))
                t = Tn(lambda E: E.transpose(p_A[0:16, 0:128], s_rsd[:, 0:16], s_eyef[:]),
                       waits=[(sv, v_nrs)], inc=stn)
                V(lambda E: E.tensor_copy(s_prsdf[:], p_A[0:16, 0:128]), waits=[(stn, t)])
                V(lambda E: E.tensor_scalar(s_prsdf[:], s_prsdf[:], 0.5,
                                            None, OP.mult))
                v_prsd = V(lambda E: E.tensor_copy(s_prsdb[:], s_prsdf[:]))
                tP1 = dma("gpsimd", q_ar, prsb_d, s_prsdb[:], waits=[(sv, v_prsd)])

                barrier()
                flush_prog(block)

            with ExitStack() as PB:
                SB = lambda n, s, dt=F32: PB.enter_context(nc.sbuf_tensor(n, s, dt))
                PS = lambda n, s: PB.enter_context(nc.psum_tensor(n, s, F32))
                s_wgu = SB("s_wgu", [128, NOB * 512], BF16)
                s_jA = SB("s_jA", [128, 512])
                s_jB = SB("s_jB", [128, 512])
                s_prsrow = SB("s_prsrow", [1, D], BF16)
                s_xld = SB("s_xld", [128, D])
                s_xn = SB("s_xn", [128, D])
                s_x1 = SB("s_x1", [128, 4 * D])
                s_hz = SB("s_hz", [128, NOB * 512], BF16)
                s_xq = SB("s_xq", [128, D], BF16)
                s_xqT = SB("s_xqT", [128, NKD * 512], BF16)
                s_run = SB("s_run", [128, 512], BF16)
                s_habs = SB("s_habs", [128, 512], BF16)
                s_sil = SB("s_sil", [128, 512])
                s_sil2 = SB("s_sil2", [128, 512])
                s_is1bc = SB("s_is1bc", [128, 512])
                s_w3bc = SB("s_w3bc", [128, 512])
                s_oc = [SB(f"s_oc{i}", [128, 512]) for i in range(2)]
                s_s1c = SB("s_s1c", [128, 4])
                s_is1c = SB("s_is1c", [128, 4])
                s_ccol = SB("s_ccol", [128, 4])
                s_dcol = SB("s_dcol", [128, 4])
                s_w3col = SB("s_w3col", [128, 4])
                s_rowt = SB("s_rowt", [4, 128])
                s_flatt = SB("s_flatt", [1, 512])
                s_vcol = SB("s_vcol", [128, 1])
                s_rcol = SB("s_rcol", [128, 1])
                s_ncol = SB("s_ncol", [128, 1])
                s_acx = SB("s_acx", [128, 1])
                s_bn = SB("s_bn", [128, 24])
                s_bag = SB("s_bag", [128, 2])
                p_gu = [PS(f"p_gu{i}", [128, 512]) for i in range(4)]
                t_prsr = dma("gpsimd", q_ar, s_prsrow[:],
                             prsb_d.rearrange("a b -> (a b)").rearrange("(o n) -> o n", o=1),
                             waits=[(q_ar, tP1)])
                p_dn = [PS(f"p_dn{i}", [128, 512]) for i in range(2)]
                p_x = PB.enter_context(nc.psum_tensor("p_x", [128, 1024], BF16))
                p_c = PS("p_c", [128, 512])

                NSLOT = 5
                slot_ld = [None] * NSLOT
                x_free = 0
                px_free = 0
                gu_free = [0, 0]
                dn_free = [0, 0]
                oc_free = [0, 0]
                w_last_mm = 0

                for st in range(NST):
                    st0 = ST * st
                    xq_last = 0
                    for t4 in range(4):
                        tok0 = st0 + 128 * t4
                        wx = [(sv, x_free)] if x_free else []
                        tX = dma("sync", q_x, s_xld[:], x_d[tok0:tok0 + 128, :],
                                 waits=wx)
                        for q in range(4):
                            V(lambda E, q=q: E.bn_stats(
                                s_bn[:, 6 * q:6 * (q + 1)],
                                s_xld[:, 512 * q:512 * (q + 1)]),
                                waits=[(q_x, tX)] if q == 0 else [])
                        V(lambda E: E.bn_aggr(s_bag[:], s_bn[:]))
                        v_ve = V(lambda E: E.tensor_scalar(
                            s_vcol[:], s_bag[:, 1:2], LN_EPS, None, OP.add))
                        s = Sc(lambda E: E.activation(s_vcol[:], s_vcol[:], AF.Sqrt),
                               waits=[(sv, v_ve)])
                        V(lambda E: E.reciprocal(s_rcol[:], s_vcol[:]),
                          waits=[(ss, s)])
                        v_ln = V(lambda E: E.scalar_tensor_tensor(
                            s_ncol[:], s_bag[:, 0:1], -1.0, s_rcol[:],
                            OP.mult, OP.mult))
                        s = Sc(lambda E: E.activation(
                            s_xn[:], s_xld[:], AF.Identity, bias=s_ncol[:, 0:1],
                            scale=s_rcol[:, 0:1]), waits=[(sv, v_ln)])
                        V(lambda E: E.tensor_tensor(s_xn[:], s_xn[:], s_gA[:],
                                                    OP.mult), waits=[(ss, s)])
                        x_free = V(lambda E, t4=t4: E.tensor_tensor(
                            s_x1[:, D * t4:D * (t4 + 1)], s_xn[:], s_xld[:], OP.add))
                        for q in range(4):
                            V(lambda E, q=q, t4=t4: E.bn_stats(
                                s_bn[:, 6 * q:6 * (q + 1)],
                                s_x1[:, D * t4 + 512 * q:D * t4 + 512 * (q + 1)]))
                        V(lambda E: E.bn_aggr(s_bag[:], s_bn[:]))
                        v_ve = V(lambda E: E.tensor_scalar(
                            s_vcol[:], s_bag[:, 1:2], LN_EPS, None, OP.add))
                        s = Sc(lambda E: E.activation(s_vcol[:], s_vcol[:], AF.Sqrt),
                               waits=[(sv, v_ve)])
                        V(lambda E: E.reciprocal(s_rcol[:], s_vcol[:]),
                          waits=[(ss, s)])
                        v_ln2 = V(lambda E: E.scalar_tensor_tensor(
                            s_ncol[:], s_bag[:, 0:1], -1.0, s_rcol[:],
                            OP.mult, OP.mult))
                        s = Sc(lambda E, t4=t4: E.activation(
                            s_xn[:], s_x1[:, D * t4:D * (t4 + 1)], AF.Identity,
                            bias=s_ncol[:, 0:1], scale=s_rcol[:, 0:1]),
                            waits=[(sv, v_ln2)])
                        V(lambda E: E.reduce_max(
                            s_acx[:], s_xn[:], axis=AX, apply_absolute_value=True),
                            waits=[(ss, s)])
                        V(lambda E: E.tensor_scalar(
                            s_acx[:], s_acx[:], Q_EPS, None, OP.max))
                        V(lambda E, t4=t4: E.tensor_scalar(
                            s_is1c[:, t4:t4 + 1], s_acx[:], 1.0 / 127.0, None,
                            OP.mult))
                        V(lambda E: E.reciprocal(s_rcol[:], s_acx[:]))
                        V(lambda E, t4=t4: E.tensor_scalar(
                            s_s1c[:, t4:t4 + 1], s_rcol[:], 127.0, None, OP.mult))
                        v_xq = 0
                        for q in range(4):
                            sl = slice(512 * q, 512 * (q + 1))
                            wq_ = [(ss, px_free)] if (q == 0 and px_free) else []
                            V(lambda E, sl=sl, t4=t4: E.tensor_scalar(
                                s_jA[:], s_xn[:, sl], s_s1c[:, t4:t4 + 1], 384.5,
                                OP.mult, OP.add), waits=wq_)
                            V(lambda E: E.tensor_scalar(
                                s_jA[:].bitcast(I32), s_jA[:].bitcast(I32),
                                MASK15, None, OP.bitwise_and))
                            v_xq = V(lambda E, sl=sl: E.tensor_scalar(
                                s_xq[:, sl], s_jA[:], -256.0, None, OP.add))
                        for nb in range(4):
                            wtx = [(sv, v_xq)] if nb == 0 else []
                            if px_free:
                                wtx = wtx + [(ss, px_free)]
                            t = 0
                            for j in range(4):
                                k = 4 * nb + j
                                t = Tn(lambda E, j=j, k=k: E.transpose(
                                    p_x[:, 128 * j:128 * (j + 1)],
                                    s_xq[:, 128 * k:128 * (k + 1)], s_eyeb[:]),
                                    waits=wtx if j == 0 else [],
                                    inc=stn if j == 3 else None)
                            px_free = Sc(lambda E, nb=nb, t4=t4: E.copy(
                                s_xqT[:].rearrange("p (k c) -> p k c", c=512)
                                [:, 4 * nb:4 * nb + 4, 128 * t4:128 * (t4 + 1)],
                                p_x[:, 0:512].rearrange("p (j c) -> p j c", c=128)),
                                waits=[(stn, t)])
                        xq_last = px_free
                    t = Tn(lambda E: E.transpose(p_c[0:4, 0:128], s_is1c[:],
                                                 s_eyef[:]),
                           waits=[(sv, sv.n)], inc=stn)
                    v_r2 = V(lambda E: E.tensor_copy(s_rowt[:], p_c[0:4, 0:128]),
                             waits=[(stn, t)])
                    tB1 = dma("gpsimd", q_ar, rowf_d, s_rowt[:], waits=[(sv, v_r2)])
                    tB2 = dma("gpsimd", q_ar, s_flatt[:],
                              rowf_d.rearrange("a b -> (a b)").rearrange(
                                  "(o n) -> o n", o=1), waits=[(q_ar, tB1)])
                    t = Tn(lambda E: E.matmul(
                        p_c[:, 0:512], s_onesf[:], s_flatt[:],
                        start=True, stop=True),
                        waits=[(q_ar, tB2), (sv, v_r2)], inc=stn)
                    v_bc = V(lambda E: E.tensor_copy(s_is1bc[:], p_c[:, 0:512]),
                             waits=[(stn, t)])

                    run_last = 0

                    def load_gu(ob, guard):
                        sl = ob % NSLOT
                        b0 = 4096 * sl
                        qg = q_wl[ob % 2]
                        qu = qc if ob % 2 == 0 else q_u2
                        return (
                            qg, dma("sync", qg, s_wgu[:, b0:b0 + 2048],
                                    wqgt_d[ob], waits=list(guard) + [(qg, qg.n)]),
                            qu, dma("scalar", qu,
                                    s_wgu[:, b0 + 2048:b0 + 4096],
                                    wqut_d[ob], waits=list(guard) + [(qu, qu.n)]))

                    for ob in range(NSLOT):
                        guard = []
                        if ob == 0:
                            guard = ([(stn, w_last_mm)] if w_last_mm else
                                     [(q, q.n) for q in (q_st[0], q_st[1])])
                        slot_ld[ob] = load_gu(ob, guard)
                    for ob in range(NOB):
                        sl = ob % NSLOT
                        b0 = 4096 * sl
                        qg_, tG_, qu_, tU_ = slot_ld[sl]
                        pgb = p_gu[2 * (ob % 2)]
                        pub = p_gu[2 * (ob % 2) + 1]
                        bw = [(qg_, tG_), (qu_, tU_), (ss, xq_last),
                              (sv, v_bc)]
                        if gu_free[ob % 2]:
                            bw.append((sv, gu_free[ob % 2]))
                        for k in range(NKD):
                            Tn(lambda E, k=k, b0=b0, pgb=pgb: E.matmul(
                                pgb[:, 0:512],
                                s_wgu[:, b0 + 128 * k:b0 + 128 * (k + 1)],
                                s_xqT[:, 512 * k:512 * (k + 1)],
                                start=(k == 0), stop=(k == NKD - 1)),
                                waits=bw if k == 0 else [])
                        for k in range(NKD):
                            w_last_mm = Tn(
                                lambda E, k=k, b0=b0, pub=pub: E.matmul(
                                    pub[:, 0:512],
                                    s_wgu[:, b0 + 2048 + 128 * k:
                                          b0 + 2048 + 128 * (k + 1)],
                                    s_xqT[:, 512 * k:512 * (k + 1)],
                                    start=(k == 0), stop=(k == NKD - 1)),
                                inc=stn if k == NKD - 1 else None)
                        t_gu = w_last_mm
                        if ob + NSLOT < NOB:
                            slot_ld[sl] = load_gu(ob + NSLOT, [(stn, t_gu)])
                        v_si = V(lambda E, ob=ob, pgb=pgb: E.scalar_tensor_tensor(
                            s_sil[:], pgb[:, 0:512], s_nrsg[:, ob:ob + 1],
                            s_is1bc[:], OP.add, OP.mult), waits=[(stn, t_gu)])
                        s = Sc(lambda E: E.activation(s_sil2[:], s_sil[:],
                                                      AF.Silu),
                               waits=[(sv, v_si)])
                        v = V(lambda E, ob=ob, pub=pub: E.scalar_tensor_tensor(
                            s_hz[:, 512 * ob:512 * (ob + 1)], pub[:, 0:512],
                            s_nrsu[:, ob:ob + 1], s_sil2[:], OP.add, OP.mult),
                            waits=[(ss, s)])
                        gu_free[ob % 2] = v
                        abw = [(sv, v)]
                        if run_last:
                            abw.append((sv, run_last))
                        s_ab = Sc(lambda E, ob=ob: E.activation(
                            s_habs[:], s_hz[:, 512 * ob:512 * (ob + 1)],
                            AF.Abs), waits=abw)
                        if ob == 0:
                            run_last = V(lambda E: E.tensor_copy(
                                s_run[:], s_habs[:]), waits=[(ss, s_ab)])
                        else:
                            run_last = V(lambda E: E.tensor_tensor(
                                s_run[:], s_run[:], s_habs[:], OP.max),
                                waits=[(ss, s_ab)])
                    t_am = 0
                    for tau in range(4):
                        wam = [(sv, run_last), (ss, px_free)] if tau == 0 else []
                        t_am = Tn(lambda E, tau=tau: E.transpose(
                            p_x[:, 128 * tau:128 * (tau + 1)],
                            s_run[:, 128 * tau:128 * (tau + 1)], s_eyeb[:]),
                            waits=wam, inc=stn if tau == 3 else None)
                    px_free = Sc(lambda E: E.copy(s_habs[:], p_x[:, 0:512]),
                                 waits=[(stn, t_am)])
                    for tau in range(4):
                        V(lambda E, tau=tau: E.reduce_max(
                            s_dcol[:, tau:tau + 1],
                            s_habs[:, 128 * tau:128 * (tau + 1)], axis=AX),
                            waits=[(ss, px_free)] if tau == 0 else ())
                    for t4 in range(4):
                        V(lambda E, t4=t4: E.tensor_scalar(
                            s_vcol[:], s_s1c[:, t4:t4 + 1], Q_EPS, None, OP.mult))
                        V(lambda E, t4=t4: E.tensor_tensor(
                            s_dcol[:, t4:t4 + 1], s_dcol[:, t4:t4 + 1],
                            s_vcol[:], OP.max))
                        V(lambda E, t4=t4: E.reciprocal(
                            s_rcol[:], s_dcol[:, t4:t4 + 1]))
                        V(lambda E, t4=t4: E.tensor_scalar(
                            s_w3col[:, t4:t4 + 1], s_rcol[:], 127.0, None,
                            OP.mult))
                        V(lambda E, t4=t4: E.scalar_tensor_tensor(
                            s_ccol[:, t4:t4 + 1], s_dcol[:, t4:t4 + 1],
                            1.0 / 127.0, s_is1c[:, t4:t4 + 1],
                            OP.mult, OP.mult))
                    t = Tn(lambda E: E.transpose(p_c[0:4, 0:128], s_w3col[:],
                                                 s_eyef[:]),
                           waits=[(sv, sv.n)], inc=stn)
                    v = V(lambda E: E.tensor_copy(s_rowt[:], p_c[0:4, 0:128]),
                          waits=[(stn, t)])
                    tB3 = dma("gpsimd", q_ar, rowf_d, s_rowt[:], waits=[(sv, v)])
                    tB4 = dma("gpsimd", q_ar, s_flatt[:],
                              rowf_d.rearrange("a b -> (a b)").rearrange(
                                  "(o n) -> o n", o=1), waits=[(q_ar, tB3)])
                    t = Tn(lambda E: E.matmul(
                        p_c[:, 0:512], s_onesf[:], s_flatt[:],
                        start=True, stop=True),
                        waits=[(q_ar, tB4), (sv, v)], inc=stn)
                    v_w3 = V(lambda E: E.tensor_copy(s_w3bc[:], p_c[:, 0:512]),
                             waits=[(stn, t)])
                    z_last = 0
                    for ob0 in range(0, NOB, 2):
                        obs = [(ob0, s_jA)] + ([(ob0 + 1, s_jB)]
                                               if ob0 + 1 < NOB else [])
                        for i, (ob, jt) in enumerate(obs):
                            V(lambda E, ob=ob, jt=jt: E.tensor_tensor(
                                jt[:], s_hz[:, 512 * ob:512 * (ob + 1)],
                                s_w3bc[:], OP.mult),
                                waits=[(sv, v_w3)] if ob == 0 else [],
                                dr=(i == len(obs) - 1))
                        for i, (ob, jt) in enumerate(obs):
                            V(lambda E, jt=jt: E.tensor_scalar(
                                jt[:], jt[:], 384.5, None, OP.add),
                                dr=(i == len(obs) - 1))
                        for i, (ob, jt) in enumerate(obs):
                            V(lambda E, jt=jt: E.tensor_scalar(
                                jt[:].bitcast(I32), jt[:].bitcast(I32),
                                MASK15, None, OP.bitwise_and),
                                dr=(i == len(obs) - 1))
                        for i, (ob, jt) in enumerate(obs):
                            z_last = V(lambda E, ob=ob, jt=jt: E.tensor_scalar(
                                s_hz[:, 512 * ob:512 * (ob + 1)], jt[:], -384.5,
                                None, OP.add), dr=(i == len(obs) - 1))
                    for dc in range(4):
                        d0 = 512 * dc
                        wdw = [(stn, w_last_mm), (q_st[0], q_st[0].n)]
                        tWD = dma("gpsimd", q_st[0],
                                  s_wgu[:, 0:NOB * 512].rearrange(
                                      "p (f j) -> p f j", j=512),
                                  wqdt_d[:, d0:d0 + 512].rearrange(
                                      "(f p) j -> p f j", p=128),
                                  waits=wdw)
                        for t4 in range(4):
                            tok0 = st0 + 128 * t4
                            pdb = p_dn[t4 % 2]
                            bw = [(q_st[0], tWD), (sv, z_last)]
                            if dn_free[t4 % 2]:
                                bw.append((sv, dn_free[t4 % 2]))
                            for f in range(NOB):
                                Tn(lambda E, f=f, t4=t4, pdb=pdb: E.matmul(
                                    pdb[:, 0:512],
                                    s_hz[:, 512 * f + 128 * t4:
                                         512 * f + 128 * (t4 + 1)],
                                    s_wgu[:, 512 * f:512 * (f + 1)],
                                    start=(f == 0), stop=False),
                                    waits=bw if f == 0 else [])
                            t_dn = 0
                            for q in range(4):
                                t_dn = Tn(lambda E, q=q, dc=dc, pdb=pdb: E.matmul(
                                    pdb[:, 128 * q:128 * (q + 1)],
                                    s_ones_row_b[:],
                                    s_prsrow[0:1, 512 * dc + 128 * q:
                                             512 * dc + 128 * (q + 1)],
                                    start=False, stop=(q == 3)),
                                    inc=stn if q == 3 else None)
                            w_last_mm = t_dn
                            par = (dc * 4 + t4) % 2
                            wv = [(stn, t_dn)]
                            if oc_free[par]:
                                wv.append((q_y[par], oc_free[par]))
                            v = V(lambda E, t4=t4, dc=dc, pdb=pdb, par=par:
                                  E.scalar_tensor_tensor(
                                      s_oc[par][:], pdb[:, 0:512],
                                      s_ccol[:, t4:t4 + 1],
                                      s_x1[:, D * t4 + 512 * dc:
                                           D * t4 + 512 * (dc + 1)],
                                      OP.mult, OP.add), waits=wv)
                            dn_free[t4 % 2] = v
                            oc_free[par] = dma(
                                "gpsimd", q_y[par], y_d[tok0:tok0 + 128, d0:d0 + 512],
                                s_oc[par][:], waits=[(sv, v)])
                    barrier()
                barrier()
                flush_prog(block)
    return nc



def kernel(L0_attn_X, L0_attn_W, L0_attn_b, L0_ffn_w_gate, L0_ffn_w_up,
           L0_ffn_w_down):
    X = np.asarray(L0_attn_X)
    W = np.asarray(L0_attn_W)
    b = np.asarray(L0_attn_b)
    wg = np.ascontiguousarray(np.asarray(L0_ffn_w_gate), np.float32)
    wu = np.ascontiguousarray(np.asarray(L0_ffn_w_up), np.float32)
    wd = np.ascontiguousarray(np.asarray(L0_ffn_w_down), np.float32)

    perm = np.concatenate([np.arange(0, 1024, 2), np.arange(1, 1024, 2),
                           np.arange(1024, 2048)])
    fperm = np.concatenate([np.concatenate([perm[128 * c:128 * (c + 1)],
                                            2048 + perm[128 * c:128 * (c + 1)]])
                            for c in range(16)])
    Wp = np.ascontiguousarray(np.asarray(W, np.float32)[:, fperm])
    bp = np.ascontiguousarray(np.asarray(b, np.float32)[fperm].reshape(1, F))
    eyeb = np.eye(128, dtype=ml_dtypes.bfloat16)
    eyef = np.eye(128, dtype=np.float32)
    jc = (np.arange(4)[None, :] * 128
          + np.arange(128)[:, None]).astype(np.float32)

    nc = build_nc()
    in_maps = []
    for c in range(NCORES):
        bc, sc = c // 4, c % 4
        xs = np.ascontiguousarray(X[bc, T * sc:T * (sc + 1), :], dtype=np.float32)
        m32 = np.zeros((64, 1), np.float32)
        m32[32 * bc:32 * bc + 16] = 1.0
        in_maps.append(dict(
            xt=np.ascontiguousarray(xs.T), x=xs, wkv=Wp, bkv=bp,
            wg=wg, wu=wu, wd=wd,
            pos=np.arange(T * sc, T * (sc + 1), dtype=np.float32).reshape(1, T),
            jc=jc, eyeb=eyeb, eyef=eyef,
            selp=np.tile(np.array([[1.0 - bc, float(bc)]], np.float32), (128, 1)),
            m32=m32,
        ))
    res = run_bass_kernel_spmd(nc, in_maps, core_ids=list(range(NCORES)))
    out = np.empty((2, 8192, 2048), np.float32)
    for c in range(NCORES):
        bc, sc = c // 4, c % 4
        out[bc, T * sc:T * (sc + 1), :] = res.results[c]["y"]
    return out


# version 13
# speedup vs baseline: 1.0954x; 1.0954x over previous
import os
import numpy as np
import ml_dtypes
import concourse.bass as bass
import concourse.mybir as mybir
import bass_rust
from concourse import library_config
from concourse.bass_utils import run_bass_kernel_spmd
from contextlib import ExitStack

F32 = mybir.dt.float32
BF16 = mybir.dt.bfloat16
OP = mybir.AluOpType
AF = mybir.ActivationFunctionType
AX = mybir.AxisListType.X
RED = bass_rust.ReduceOp

PI = np.pi
TWO_PI = 2.0 * np.pi
CLAMP = float(np.float32(3.1415925))
INV2PI = 1.0 / (2.0 * np.pi)
CW1 = 6.283203125
CW2 = float(2.0 * np.pi - 6.283203125)
I32 = mybir.dt.int32
MASK12 = -4096
MASK13 = -8192
MASK15 = -32768
LN_EPS = 1e-5
Q_EPS = 1e-5

D = 2048
F = 4096
FF = 5504
T = 2048
TG = 256
NTG = T // TG
ST = 512
NST = T // ST
NKD = D // 128
NFC = F // 128
NOB = FF // 128
NCORES = 8

ENGINES = ("sync", "scalar", "vector", "tensor", "gpsimd")
SIM_DRAINS = os.environ.get("BASS_SIM_FP64") == "1" or     os.environ.get("BASS_FORCE_DRAINS") == "1"


class Ctr:
    def __init__(self, handle, name):
        self.h = handle
        self.name = name
        self.n = 0

    def inc(self, k=1):
        self.n += k
        return self.n


def build_nc(replica_groups=None):
    if replica_groups is None:
        replica_groups = [list(range(NCORES))]
    nc = bass.Bass("TRN2", target_bir_lowering=False, debug=False)

    xt_d = nc.dram_tensor("xt", [D, T], F32, kind="ExternalInput").ap()
    x_d = nc.dram_tensor("x", [T, D], F32, kind="ExternalInput").ap()
    wkv_d = nc.dram_tensor("wkv", [D, F], F32, kind="ExternalInput").ap()
    bkv_d = nc.dram_tensor("bkv", [1, F], F32, kind="ExternalInput").ap()
    wg_d = nc.dram_tensor("wg", [FF, D], F32, kind="ExternalInput").ap()
    wu_d = nc.dram_tensor("wu", [FF, D], F32, kind="ExternalInput").ap()
    wd_d = nc.dram_tensor("wd", [D, FF], F32, kind="ExternalInput").ap()
    pos_d = nc.dram_tensor("pos", [1, T], F32, kind="ExternalInput").ap()
    jc_d = nc.dram_tensor("jc", [128, 4], F32, kind="ExternalInput").ap()
    eyeb_d = nc.dram_tensor("eyeb", [128, 128], BF16, kind="ExternalInput").ap()
    eyef_d = nc.dram_tensor("eyef", [128, 128], F32, kind="ExternalInput").ap()
    selp_d = nc.dram_tensor("selp", [128, 2], F32, kind="ExternalInput").ap()
    m32_d = nc.dram_tensor("m32", [64, 1], F32, kind="ExternalInput").ap()
    y_d = nc.dram_tensor("y", [T, D], F32, kind="ExternalOutput").ap()

    wbs_d = nc.dram_tensor("wbs", [NFC, D, 128], BF16).ap()
    wqgt_d = nc.dram_tensor("wqgt", [NOB, D, 128], BF16).ap()
    wqut_d = nc.dram_tensor("wqut", [NOB, D, 128], BF16).ap()
    wqdt_d = nc.dram_tensor("wqdt", [FF, D], BF16).ap()
    arin_d = nc.dram_tensor("arin", [64, 128], F32).ap()
    arout_d = nc.dram_tensor("arout", [64, 128], F32).ap()
    rowf_d = nc.dram_tensor("rowf", [4, 128], F32).ap()
    prsb_d = nc.dram_tensor("prsb", [16, 128], BF16).ap()
    csb_d = nc.dram_tensor("csb", [32, 128], F32).ap()

    with ExitStack() as CTX:
        SEM = lambda n: Ctr(CTX.enter_context(nc.semaphore(n)), n)
        q_tiny = SEM("q_tiny")
        q_xt = [SEM("q_xt0"), SEM("q_xt1")]
        q_wl = [SEM("q_wl0"), SEM("q_wl1")]
        q_wf = [SEM("q_wf0"), SEM("q_wf1"), SEM("q_wf2"), SEM("q_wf3")]
        q_x = SEM("q_x")
        q_wbs = SEM("q_wbs")
        q_st = [SEM("q_st0"), SEM("q_st1")]
        q_ar = SEM("q_ar")
        q_y = [SEM("q_y0"), SEM("q_y1")]
        qc = SEM("qc")
        q_u2 = SEM("q_u2")
        SYNC_QS = [q_tiny, q_xt[0], q_xt[1], q_wl[0], q_wl[1], q_x]
        GP_QS = [q_wbs, q_st[0], q_st[1], q_ar, q_y[0], q_y[1], q_wf[1],
                 q_wf[3]]
        SC_QS = [qc, q_u2, q_wf[0], q_wf[2]]
        sv = SEM("sv")
        ss = SEM("ss")
        stn = SEM("stn")
        sg = SEM("sg")
        scc = SEM("scc")
        bar = SEM("bar")

        prog = {e: [] for e in ENGINES}

        def mk(engine, fn, waits=(), inc=None, k=1, drain=False):
            for (c, v) in waits:
                assert v <= c.n, f"[{engine}] wait {c.name}>={v} but built {c.n}"
            tgt = inc.inc(k) if inc is not None else None

            def go(E, fn=fn, waits=tuple(waits), inc=inc, k=k, drain=drain):
                for (c, v) in waits:
                    if v > 0:
                        E.wait_ge(c.h, v)
                r = fn(E)
                if inc is not None:
                    r.then_inc(inc.h, k)
                if drain:
                    E.drain()
            prog[engine].append(go)
            return tgt

        V = lambda fn, waits=(), inc=sv, k=1, dr=True: mk("vector", fn, waits, inc, k, drain=dr)
        Sc = lambda fn, waits=(), inc=ss, k=1, dr=True: mk("scalar", fn, waits, inc, k, drain=dr)
        Tn = lambda fn, waits=(), inc=None, k=1: mk("tensor", fn, waits, inc, k)
        G = lambda fn, waits=(), inc=sg, k=1: mk("gpsimd", fn, waits, inc, k)

        def dma(qeng, qctr, out, in_, waits=()):
            return mk(qeng, lambda E, o=out, i=in_: E.dma_start(out=o, in_=i),
                      waits, qctr, 16)

        def drain_nop(E):
            E.drain()
            return E.nop()

        def barrier():
            base = bar.n
            mk("sync", lambda E: E.nop(),
               waits=[(q, q.n) for q in SYNC_QS], inc=bar)
            mk("scalar", drain_nop, waits=[(q, q.n) for q in SC_QS], inc=bar)
            mk("vector", drain_nop, waits=(), inc=bar)
            mk("tensor", lambda E: E.nop(), waits=(), inc=bar)
            mk("gpsimd", lambda E: E.nop(),
               waits=[(q, q.n) for q in GP_QS], inc=bar)
            tgt = base + 5
            for e in ENGINES:
                mk(e, lambda E: E.nop(), waits=[(bar, tgt)])
            return tgt

        def flush_prog(block):
            for e in ENGINES:
                ops = tuple(prog[e])
                prog[e] = []
                if not ops:
                    continue

                def body(E, ops=ops):
                    for fn in ops:
                        fn(E)
                getattr(block, e)(body)

        SBP = lambda n, s, dt=F32: CTX.enter_context(nc.sbuf_tensor(n, s, dt))
        s_invf = SBP("s_invf", [128, 4])
        s_jc = SBP("s_jc", [128, 4])
        s_eyeb = SBP("s_eyeb", [128, 128], BF16)
        s_eyef = SBP("s_eyef", [128, 128])
        s_onesb = SBP("s_onesb", [128, 1], BF16)
        s_ones_row_b = SBP("s_ones_row_b", [1, 128], BF16)
        s_onesf = SBP("s_onesf", [1, 128])
        s_gA = SBP("s_gA", [128, D])
        s_A = SBP("s_A", [128, 16])
        s_acol = SBP("s_acol", [128, 2])
        s_rsg = SBP("s_rsg", [128, NOB])
        s_rsu = SBP("s_rsu", [128, NOB])
        s_rsd = SBP("s_rsd", [128, 16])
        s_nrsg = SBP("s_nrsg", [128, NOB])
        s_nrsu = SBP("s_nrsu", [128, NOB])
        s_prsdf = SBP("s_prsdf", [16, 128])
        s_prsdb = SBP("s_prsdb", [16, 128], BF16)
        s_rs1 = SBP("s_rs1", [128, 1])
        s_rsacc = SBP("s_rsacc", [128, 1])

        with nc.Block() as block:
            with ExitStack() as PA:
                SB = lambda n, s, dt=F32: PA.enter_context(nc.sbuf_tensor(n, s, dt))
                PS = lambda n, s: PA.enter_context(nc.psum_tensor(n, s, F32))

                s_wld = [SB(f"s_wld{i}", [128, D]) for i in range(2)]

                s_q1 = SB("s_q1", [128, D], BF16)
                s_q2 = SB("s_q2", [128, D], BF16)
                s_wqo = SB("s_wqo", [128, D], BF16)
                s_red = SB("s_red", [128, 16])
                s_scale = SB("s_scale", [128, 16])
                s_nscale = SB("s_nscale", [128, 16])
                s_half = SB("s_half", [128, 16])
                s_nhalf = SB("s_nhalf", [128, 16])
                s_stripe = [SB(f"s_stripe{i}", [128, D], BF16) for i in range(2)]
                s_posb = SB("s_posb", [128, T])
                s_brow = SB("s_brow", [1, F])
                s_posr = SB("s_posr", [1, T])
                s_r0 = SB("s_r0", [1, D])
                s_r1 = SB("s_r1", [1, D])
                s_csw = SB("s_csw", [1, F])
                s_cswT = SB("s_cswT", [32, 128])
                s_cswc = SB("s_cswc", [128, 32])
                s_ar2 = SB("s_ar2", [64, 128])
                s_arin = SB("s_arin", [64, 128])
                s_m32 = SB("s_m32", [64, 1])
                s_selp = SB("s_selp", [128, 2])
                s_xtld = [SB(f"s_xtld{i}", [128, TG]) for i in range(2)]
                s_xtb = SB("s_xtb", [128, NKD * TG], BF16)
                s_sq = SB("s_sq", [128, TG], BF16)
                s_kvt = SB("s_kvt", [128, NFC * TG])
                s_wbf = s_kvt[:, 0:F // 2].bitcast(BF16)
                s_wf = [SB(f"s_wf{i}", [128, NKD * 128], BF16) for i in range(4)]
                TG2 = 2 * TG
                s_tabc = SB("s_tabc", [128, TG2])
                s_tabs = SB("s_tabs", [128, TG2])
                s_frq = SB("s_frq", [128, TG2])
                s_u1 = SB("s_u1", [128, TG2])
                s_u2 = SB("s_u2", [128, TG2])
                s_u3 = SB("s_u3", [128, TG2])
                s_u4 = SB("s_u4", [128, TG2])
                s_tmA = SB("s_tmA", [128, TG2])
                s_tmB = SB("s_tmB", [128, TG2])
                s_tmC = SB("s_tmC", [128, TG2])
                s_tmD = SB("s_tmD", [128, TG2])
                s_jk = SB("s_jk", [128, TG])
                s_mrow = SB("s_mrow", [1, TG])
                s_nmrow = SB("s_nmrow", [1, TG])
                s_varow = SB("s_varow", [1, TG])
                s_sdrow = SB("s_sdrow", [1, TG])
                s_rstdrow = SB("s_rstdrow", [1, TG])
                s_rbc2 = SB("s_rbc2", [128, 2 * TG])

                p_kv = [PS(f"p_kv{i}", [128, 512]) for i in range(4)]
                p_row = PS("p_row", [1, 512])
                p_tr = [PA.enter_context(nc.psum_tensor(f"p_tr{i}", [128, 1024], BF16))
                        for i in range(2)]
                p_A2 = PS("p_A2", [128, 512])
                p_A = p_A2

                for _dst, _src in [(s_jc[:], jc_d), (s_eyeb[:], eyeb_d),
                                   (s_eyef[:], eyef_d), (s_brow[:], bkv_d),
                                   (s_selp[:], selp_d), (s_m32[:], m32_d),
                                   (s_posr[:], pos_d)]:
                    dma("sync", q_tiny, _dst, _src)
                T_TINY = q_tiny.n

                V(lambda E: E.memset(s_onesb[:], 1.0))
                V(lambda E: E.memset(s_ones_row_b[:], 1.0))
                v_onef = V(lambda E: E.memset(s_onesf[:], 1.0))
                V(lambda E: E.memset(s_A[:], 0.0))
                v_pb = 0
                for j in range(4):
                    wpb = [(q_tiny, T_TINY), (sv, v_onef)]
                    if v_pb:
                        wpb.append((sv, v_pb))
                    t = Tn(lambda E, j=j: E.matmul(
                        p_kv[1][:, 0:512], s_onesf[:],
                        s_posr[0:1, 512 * j:512 * (j + 1)],
                        start=True, stop=True), waits=wpb, inc=stn)
                    v_pb = V(lambda E, j=j: E.tensor_copy(
                        s_posb[:, 512 * j:512 * (j + 1)], p_kv[1][:, 0:512]),
                        waits=[(stn, t)])
                g_posb = 0
                cexp = -np.log(10000.0) / 512.0
                v_jc = V(lambda E: E.tensor_scalar(s_invf[:], s_jc[:], cexp, None, OP.mult),
                         waits=[(q_tiny, T_TINY)])
                s_if = Sc(lambda E: E.activation(s_invf[:], s_invf[:], AF.Exp),
                          waits=[(sv, v_jc)])

                wld_free = [0, 0]
                last_wbs_store = 0
                csw_mm = 0
                for k in range(NKD):
                    tw0 = dma("sync", q_wl[0], s_wld[0][:],
                              wkv_d[128 * k:128 * (k + 1), 0:D],
                              waits=[(sv, wld_free[0])])
                    tw1 = dma("sync", q_wl[1], s_wld[1][:],
                              wkv_d[128 * k:128 * (k + 1), D:F],
                              waits=[(sv, wld_free[1])])
                    st_wait = [(q_wl[0], tw0)]
                    if last_wbs_store:
                        st_wait.append((q_wbs, last_wbs_store))
                    if csw_mm:
                        st_wait.append((stn, csw_mm))
                    v0 = V(lambda E: E.tensor_copy(s_wbf[:, 0:D], s_wld[0][:]),
                           waits=st_wait)
                    v1 = V(lambda E: E.tensor_copy(s_wbf[:, D:F], s_wld[1][:]),
                           waits=[(q_wl[1], tw1)])
                    wld_free = [v0, v1]
                    ap = wbs_d.rearrange("f d j -> d f j")[128 * k:128 * (k + 1), :, :]
                    last_wbs_store = dma(
                        "gpsimd", q_wbs, ap,
                        s_wbf[:].rearrange("d (f j) -> d f j", j=128),
                        waits=[(sv, v1)])
                    for fc in range(NFC):
                        csw_mm = Tn(lambda E, fc=fc, k=k: E.matmul(
                            p_A2[:, fc:fc + 1],
                            s_wbf[:, 128 * fc:128 * (fc + 1)], s_onesb[:],
                            start=(k == 0 and fc == 0),
                            stop=(k == NKD - 1 and fc == NFC - 1)),
                            waits=[(sv, v1)] if fc == 0 else [],
                            inc=stn if (fc == NFC - 1) else None)
                v_c1 = V(lambda E: E.tensor_copy(s_cswc[:], p_A2[:, 0:32]),
                         waits=[(stn, csw_mm)])
                t_cswT = Tn(lambda E: E.transpose(
                    p_kv[0][0:32, 0:128], s_cswc[:], s_eyef[:]),
                    waits=[(sv, v_c1), (q_tiny, T_TINY)], inc=stn)
                v_c2 = V(lambda E: E.tensor_copy(s_cswT[:], p_kv[0][0:32, 0:128]),
                         waits=[(stn, t_cswT)])
                t_cb1 = dma("gpsimd", q_ar, csb_d, s_cswT[:], waits=[(sv, v_c2)])
                t_csw = dma("gpsimd", q_ar, s_csw[:],
                            csb_d.rearrange("a b -> (a b)").rearrange(
                                "(o n) -> o n", o=1), waits=[(q_ar, t_cb1)])

                qtiles = []
                for ob in range(NOB):
                    qtiles.append(("g", ob, 0, D))
                    qtiles.append(("u", ob, 0, D))
                for ob in range(16):
                    qtiles.append(("d", ob, 0, D))
                    qtiles.append(("d", ob, D, D))
                    qtiles.append(("d", ob, 2 * D, FF - 2 * D))
                qsrc = {"g": wg_d, "u": wu_d, "d": wd_d}
                qdst = {"g": wqgt_d, "u": wqut_d, "d": wqdt_d}
                qst = {"wld": list(wld_free), "stripe": [0, 0], "pt": [0, 0],
                       "i": 0, "wqo_t": 0}

                def emit_qtile():
                    i = qst["i"]
                    if i >= len(qtiles):
                        return
                    qst["i"] += 1
                    mat, ob, i0, wid = qtiles[i]
                    G_ = wid // 128
                    buf = i % 2
                    tL = dma("sync", q_wl[buf], s_wld[buf][:, 0:wid],
                             qsrc[mat][128 * ob:128 * (ob + 1), i0:i0 + wid],
                             waits=[(sv, qst["wld"][buf])])
                    V(lambda E, buf=buf, wid=wid, G_=G_: E.reduce_sum(
                        s_red[:, 0:G_],
                        s_wld[buf][:, 0:wid].rearrange("p (g k) -> p g k", k=128),
                        axis=AX, apply_absolute_value=True),
                        waits=[(q_wl[buf], tL)])
                    V(lambda E, G_=G_: E.tensor_scalar(
                        s_scale[:, 0:G_], s_red[:, 0:G_], 1.0 / 128.0, Q_EPS,
                        OP.mult, OP.add))
                    V(lambda E, G_=G_: E.tensor_scalar(
                        s_half[:, 0:G_], s_scale[:, 0:G_], 0.5, None, OP.mult),
                        dr=False)
                    V(lambda E, G_=G_: E.tensor_scalar(
                        s_nhalf[:, 0:G_], s_scale[:, 0:G_], -0.5, None, OP.mult),
                        dr=False)
                    V(lambda E, G_=G_: E.tensor_scalar(
                        s_nscale[:, 0:G_], s_scale[:, 0:G_], -1.0, None, OP.mult))
                    v = 0
                    for g in range(G_):
                        sl = slice(128 * g, 128 * (g + 1))
                        V(lambda E, buf=buf, g=g, sl=sl: E.tensor_scalar(
                            s_q1[:, sl], s_wld[buf][:, sl], s_half[:, g:g + 1],
                            s_scale[:, g:g + 1], OP.is_gt, OP.mult), dr=False)
                        v = V(lambda E, buf=buf, g=g, sl=sl: E.tensor_scalar(
                            s_q2[:, sl], s_wld[buf][:, sl], s_nhalf[:, g:g + 1],
                            s_nscale[:, g:g + 1], OP.is_lt, OP.mult),
                            dr=(g == G_ - 1))
                    qst["wld"][buf] = v
                    v = V(lambda E, wid=wid: E.scalar_tensor_tensor(
                        s_wqo[:, 0:wid], s_q1[:, 0:wid], 1.0, s_q2[:, 0:wid],
                        OP.mult, OP.add, accum_out=s_rs1[:]),
                        waits=[(stn, qst["wqo_t"])] if qst["wqo_t"] else [])
                    if mat == "d":
                        if i0 == 0:
                            v = V(lambda E: E.tensor_copy(s_rsacc[:], s_rs1[:]))
                        else:
                            v = V(lambda E: E.tensor_tensor(
                                s_rsacc[:], s_rsacc[:], s_rs1[:], OP.add))
                        if i0 == 2 * D:
                            v = V(lambda E, ob=ob: E.tensor_copy(
                                s_rsd[:, ob:ob + 1], s_rsacc[:]))
                    else:
                        dst = s_rsg if mat == "g" else s_rsu
                        v = V(lambda E, dst=dst, ob=ob: E.tensor_copy(
                            dst[:, ob:ob + 1], s_rs1[:]))
                    vq = v
                    nbank = (G_ + 3) // 4
                    sev = 0
                    sb_ = i % 2
                    for nb in range(nbank):
                        bank = nb % 2
                        nblk = min(4, G_ - 4 * nb)
                        wtx = [(sv, vq), (q_tiny, T_TINY)] if nb == 0 else []
                        if qst["pt"][bank]:
                            wtx = wtx + [(ss, qst["pt"][bank])]
                        t = 0
                        for j in range(nblk):
                            g = 4 * nb + j
                            t = Tn(lambda E, bank=bank, j=j, g=g: E.transpose(
                                p_tr[bank][:, 128 * j:128 * (j + 1)],
                                s_wqo[:, 128 * g:128 * (g + 1)], s_eyeb[:]),
                                waits=wtx if j == 0 else [],
                                inc=stn if j == nblk - 1 else None)
                        wse = [(stn, t)]
                        if nb == 0 and qst["stripe"][sb_]:
                            wse.append((q_st[sb_], qst["stripe"][sb_]))
                        sev = Sc(lambda E, bank=bank, nb=nb, nblk=nblk, sb_=sb_: E.copy(
                            s_stripe[sb_][:, 512 * nb:512 * nb + 128 * nblk],
                            p_tr[bank][:, 0:128 * nblk]), waits=wse)
                        qst["pt"][bank] = sev
                        qst["wqo_t"] = t
                    if mat == "d":
                        ap = qdst[mat][i0:i0 + wid,
                                       128 * ob:128 * (ob + 1)].rearrange(
                            "(g k) j -> k g j", k=128)
                        src = s_stripe[sb_][:, 0:wid].rearrange(
                            "k (g j) -> k g j", j=128)
                    else:
                        ap = qdst[mat][ob]
                        src = s_stripe[sb_][:, 0:wid]
                    qst["stripe"][sb_] = dma(
                        "gpsimd", q_st[sb_], ap, src, waits=[(ss, sev)])

                kv_evac = {}
                bank_free = [v_c2, v_pb, 0, 0]
                wf_free = [0, 0, 0, 0]
                prod_last = 0
                xtb_free_t = 0
                prow_free_v = 0
                rbc_prev = 0
                tab_prev = 0

                for tg in range(NTG):
                    c0 = TG * tg
                    v_cvt = 0
                    tprev = [0, 0]
                    for k in range(NKD):
                        lb = k % 2
                        wxt = [(sv, tprev[lb])] if tprev[lb] else []
                        tX = dma("sync", q_xt[lb], s_xtld[lb][:],
                                 xt_d[128 * k:128 * (k + 1), c0:c0 + TG], waits=wxt)
                        wv = [(q_xt[lb], tX)]
                        if k == 0 and xtb_free_t:
                            wv.append((stn, xtb_free_t))
                        v_cvt = V(lambda E, k=k, lb=lb: E.tensor_copy(
                            s_xtb[:, TG * k:TG * (k + 1)], s_xtld[lb][:]), waits=wv)
                        tprev[lb] = v_cvt
                    wsx = [(sv, v_cvt), (sv, prow_free_v)]
                    t_sx = 0
                    for k in range(NKD):
                        t_sx = Tn(lambda E, k=k: E.matmul(
                            p_row[0:1, 0:TG], s_onesb[:],
                            s_xtb[:, TG * k:TG * (k + 1)],
                            start=(k == 0), stop=(k == NKD - 1)),
                            waits=wsx if k == 0 else [],
                            inc=stn if k == NKD - 1 else None)
                    prev_mm = 0
                    for k in range(NKD):
                        wsq = [(sv, v_cvt)] if k == 0 else [(stn, prev_mm)]
                        s = Sc(lambda E, k=k: E.activation(
                            s_sq[:], s_xtb[:, TG * k:TG * (k + 1)], AF.Square),
                            waits=wsq)
                        prev_mm = Tn(lambda E, k=k: E.matmul(
                            p_row[0:1, 256:256 + TG], s_onesb[:], s_sq[:],
                            start=(k == 0), stop=(k == NKD - 1)),
                            waits=[(ss, s)], inc=stn)
                    t_sxx = prev_mm
                    V(lambda E: E.tensor_scalar(
                        s_mrow[:], p_row[0:1, 0:TG], 1.0 / D, None, OP.mult),
                        waits=[(stn, t_sx), (stn, t_sxx)])
                    V(lambda E: E.scalar_tensor_tensor(
                        s_varow[:], s_mrow[:], -1.0, s_mrow[:], OP.mult, OP.mult))
                    V(lambda E: E.scalar_tensor_tensor(
                        s_varow[:], p_row[0:1, 256:256 + TG], 1.0 / D, s_varow[:],
                        OP.mult, OP.add))
                    v_ve = V(lambda E: E.tensor_scalar(
                        s_varow[:], s_varow[:], LN_EPS, None, OP.add))
                    prow_free_v = v_ve
                    s = Sc(lambda E: E.activation(s_sdrow[:], s_varow[:], AF.Sqrt),
                           waits=[(sv, v_ve)])
                    V(lambda E: E.reciprocal(s_rstdrow[:], s_sdrow[:]),
                      waits=[(ss, s)])
                    V(lambda E: E.tensor_scalar(
                        s_mrow[:], s_mrow[:], -1.0, None, OP.mult))
                    v_rows = V(lambda E: E.tensor_copy(s_nmrow[:], s_mrow[:]))
                    wrb = [(sv, v_rows), (sv, v_c2)]
                    if rbc_prev:
                        wrb.append((sv, rbc_prev))
                    t_rb = Tn(lambda E: E.matmul(
                        p_A2[:, 256:256 + TG], s_onesf[:], s_rstdrow[:],
                        start=True, stop=True), waits=wrb, inc=stn)
                    V(lambda E: E.tensor_copy(
                        s_rbc2[:, 0:TG], p_A2[:, 256:256 + TG]),
                        waits=[(stn, t_rb)], dr=False)
                    v_rb = V(lambda E: E.tensor_copy(
                        s_rbc2[:, TG:2 * TG], p_A2[:, 256:256 + TG]))
                    rbc_prev = v_rb

                    for fc in range(NFC):
                        wbuf = fc % 4
                        pr, half = fc // 2, fc % 2
                        wws = ([(stn, wf_free[wbuf])] if wf_free[wbuf]
                               else [(q_wbs, last_wbs_store)])
                        bank = pr % 4
                        tW = dma("scalar" if wbuf % 2 == 0 else "gpsimd",
                                 q_wf[wbuf],
                                 s_wf[wbuf][:].rearrange("p (k j) -> p k j", j=128),
                                 wbs_d[fc, :, :].rearrange("(k p) j -> p k j", p=128),
                                 waits=wws)
                        bw = [(q_wf[wbuf], tW), (sv, v_cvt), (q_ar, t_csw),
                              (sv, v_rows)]
                        if half == 0 and bank_free[bank]:
                            bw.append((sv, bank_free[bank]))
                        ps = p_kv[bank][:, TG * half:TG * (half + 1)]
                        for k in range(NKD):
                            Tn(lambda E, k=k, wbuf=wbuf, ps=ps, half=half:
                               E.matmul(
                                   ps, s_wf[wbuf][:, 128 * k:128 * (k + 1)],
                                   s_xtb[:, TG * k:TG * (k + 1)],
                                   start=(k == 0 and half == 0), stop=False),
                               waits=bw if k == 0 else [])
                        Tn(lambda E, fc=fc, ps=ps: E.matmul(
                            ps, s_csw[0:1, 128 * fc:128 * (fc + 1)],
                            s_nmrow[:], start=False, stop=False))
                        t_f = Tn(lambda E, fc=fc, ps=ps, half=half: E.matmul(
                            ps, s_brow[0:1, 128 * fc:128 * (fc + 1)],
                            s_sdrow[:], start=False, stop=(half == 1)), inc=stn)
                        wf_free[wbuf] = t_f
                        if half == 1:
                            v = V(lambda E, pr=pr, bank=bank: E.tensor_tensor(
                                s_kvt[:, TG2 * pr:TG2 * (pr + 1)],
                                p_kv[bank][:, 0:TG2], s_rbc2[:], OP.mult),
                                waits=[(stn, t_f), (sv, v_rb)])
                            bank_free[bank] = v
                            kv_evac[pr] = v
                    xtb_free_t = max(wf_free)

                    for c in range(4):
                        V(lambda E, c=c, c0=c0: E.tensor_scalar(
                            s_frq[:, 0:TG], s_posb[:, c0:c0 + TG],
                            s_invf[:, c:c + 1], None, OP.mult),
                            waits=[(sv, v_pb), (ss, s_if)], dr=False)
                        V(lambda E, c=c, c0=c0: E.tensor_scalar(
                            s_frq[:, TG:TG2], s_posb[:, c0:c0 + TG],
                            s_invf[:, c:c + 1], None, OP.mult))
                        def wrap_pair(srcs, qshs, pshs, un, rn, wait0):
                            V(lambda E: E.tensor_scalar(
                                un[0][:], srcs[0](), INV2PI, 2048.5 + qshs[0],
                                OP.mult, OP.add), waits=wait0, dr=False)
                            V(lambda E: E.tensor_scalar(
                                un[1][:], srcs[1](), INV2PI, 2048.5 + qshs[1],
                                OP.mult, OP.add))
                            for j in (0, 1):
                                V(lambda E, j=j: E.tensor_scalar(
                                    un[j][:].bitcast(I32), un[j][:].bitcast(I32),
                                    MASK12, None, OP.bitwise_and), dr=(j == 1))
                            for j in (0, 1):
                                V(lambda E, j=j: E.tensor_scalar(
                                    un[j][:], un[j][:], -2048.0, None, OP.add),
                                    dr=(j == 1))
                            for j in (0, 1):
                                V(lambda E, j=j: E.scalar_tensor_tensor(
                                    rn[j][:], un[j][:], -CW1, srcs[j](),
                                    OP.mult, OP.add), dr=(j == 1))
                            for j in (0, 1):
                                V(lambda E, j=j: E.scalar_tensor_tensor(
                                    rn[j][:], un[j][:], -CW2, rn[j][:],
                                    OP.mult, OP.add), dr=(j == 1))
                            vlast = 0
                            if SIM_DRAINS:
                                for j in (0, 1):
                                    V(lambda E, j=j: E.tensor_scalar(
                                        rn[j][:], rn[j][:], pshs[j], CLAMP,
                                        OP.add, OP.min), dr=(j == 1))
                                for j in (0, 1):
                                    vlast = V(lambda E, j=j: E.tensor_scalar(
                                        rn[j][:], rn[j][:], -CLAMP, None,
                                        OP.max), dr=(j == 1))
                            else:
                                for j in (0, 1):
                                    vlast = V(lambda E, j=j: E.tensor_scalar(
                                        rn[j][:], rn[j][:], pshs[j], None,
                                        OP.add), dr=(j == 1))
                            return vlast

                        v2_ = wrap_pair(
                            (lambda: s_frq[:], lambda: s_frq[:]),
                            (0.25, 0.0), (PI / 2, 0.0),
                            (s_u1, s_u3), (s_u2, s_u4),
                            [(ss, tab_prev)] if tab_prev else ())
                        Sc(lambda E: E.activation(s_tabc[:], s_u2[:], AF.Sin),
                           waits=[(sv, v2_)], dr=False)
                        s_tab = Sc(lambda E: E.activation(
                            s_tabs[:], s_u4[:], AF.Sin))
                        tab_prev = s_tab
                        pE, pO = c, c + 4
                        we = [(ss, s_tab), (sv, kv_evac[pE]), (sv, kv_evac[pO])]
                        V(lambda E, pE=pE: E.tensor_tensor(
                            s_tmA[:], s_kvt[:, TG2 * pE:TG2 * (pE + 1)],
                            s_tabc[:], OP.mult), waits=we, dr=False)
                        V(lambda E, pO=pO: E.tensor_tensor(
                            s_tmB[:], s_kvt[:, TG2 * pO:TG2 * (pO + 1)],
                            s_tabs[:], OP.mult), dr=False)
                        V(lambda E, pO=pO: E.tensor_tensor(
                            s_tmC[:], s_kvt[:, TG2 * pO:TG2 * (pO + 1)],
                            s_tabc[:], OP.mult), dr=False)
                        V(lambda E, pE=pE: E.tensor_tensor(
                            s_tmD[:], s_kvt[:, TG2 * pE:TG2 * (pE + 1)],
                            s_tabs[:], OP.mult))
                        v = V(lambda E, pE=pE: E.tensor_tensor(
                            s_kvt[:, TG2 * pE:TG2 * (pE + 1)], s_tmA[:], s_tmB[:],
                            OP.subtract), dr=False)
                        v = V(lambda E, pO=pO: E.tensor_tensor(
                            s_kvt[:, TG2 * pO:TG2 * (pO + 1)], s_tmC[:],
                            s_tmD[:], OP.add))
                        kv_evac[pE] = v
                        kv_evac[pO] = v
                    cos_prev = 0
                    for cp in range(8):
                        cE, cO = 2 * cp, 2 * cp + 1
                        wk = [(sv, kv_evac[cE]), (sv, kv_evac[cO])]
                        if cos_prev:
                            wk.append((ss, cos_prev))
                        V(lambda E, c=cE: E.tensor_scalar(
                            s_u1[:], s_kvt[:, TG2 * c:TG2 * (c + 1)], INV2PI,
                            1536.75, OP.mult, OP.add), waits=wk, dr=False)
                        V(lambda E, c=cO: E.tensor_scalar(
                            s_u3[:], s_kvt[:, TG2 * c:TG2 * (c + 1)], INV2PI,
                            1536.75, OP.mult, OP.add))
                        for u in (s_u1, s_u3):
                            V(lambda E, u=u: E.tensor_scalar(
                                u[:].bitcast(I32), u[:].bitcast(I32),
                                MASK13, None, OP.bitwise_and), dr=(u is s_u3))
                        for u in (s_u1, s_u3):
                            V(lambda E, u=u: E.tensor_scalar(
                                u[:], u[:], -1536.0, None, OP.add),
                                dr=(u is s_u3))
                        for (u, r, c) in ((s_u1, s_u2, cE), (s_u3, s_u4, cO)):
                            V(lambda E, u=u, r=r, c=c: E.scalar_tensor_tensor(
                                r[:], u[:], -CW1,
                                s_kvt[:, TG2 * c:TG2 * (c + 1)], OP.mult, OP.add),
                                dr=(u is s_u3))
                        for (u, r) in ((s_u1, s_u2), (s_u3, s_u4)):
                            V(lambda E, u=u, r=r: E.scalar_tensor_tensor(
                                r[:], u[:], -CW2, r[:], OP.mult, OP.add),
                                dr=(u is s_u3))
                        if SIM_DRAINS:
                            for r in (s_u2, s_u4):
                                V(lambda E, r=r: E.tensor_scalar(
                                    r[:], r[:], PI / 2, CLAMP, OP.add, OP.min),
                                    dr=(r is s_u4))
                            for r in (s_u2, s_u4):
                                v2_ = V(lambda E, r=r: E.tensor_scalar(
                                    r[:], r[:], -CLAMP, None, OP.max),
                                    dr=(r is s_u4))
                        else:
                            for r in (s_u2, s_u4):
                                v2_ = V(lambda E, r=r: E.tensor_scalar(
                                    r[:], r[:], PI / 2, None, OP.add),
                                    dr=(r is s_u4))
                        Sc(lambda E: E.activation(s_tmA[:], s_u2[:], AF.Sin),
                           waits=[(sv, v2_)], dr=False)
                        cos_prev = Sc(lambda E: E.activation(
                            s_tmB[:], s_u4[:], AF.Sin))
                        V(lambda E: E.scalar_tensor_tensor(
                            s_jk[:], s_tmA[:, 0:TG], 1.0, s_tmA[:, TG:TG2],
                            OP.mult, OP.mult, accum_out=s_acol[:, 0:1]),
                            waits=[(ss, cos_prev)], dr=False)
                        V(lambda E: E.scalar_tensor_tensor(
                            s_frq[:, 0:TG], s_tmB[:, 0:TG], 1.0,
                            s_tmB[:, TG:TG2], OP.mult, OP.mult,
                            accum_out=s_acol[:, 1:2]))
                        V(lambda E, c=cE: E.tensor_tensor(
                            s_A[:, c:c + 1], s_A[:, c:c + 1], s_acol[:, 0:1],
                            OP.add), dr=False)
                        prod_last = V(lambda E, c=cO: E.tensor_tensor(
                            s_A[:, c:c + 1], s_A[:, c:c + 1], s_acol[:, 1:2],
                            OP.add))
                    for _ in range(17):
                        emit_qtile()
                    barrier()
                while qst["i"] < len(qtiles):
                    emit_qtile()

                tA = Tn(lambda E: E.transpose(p_A[0:16, 0:128], s_A[:, 0:16],
                                              s_eyef[:]),
                        waits=[(sv, prod_last), (q_tiny, T_TINY)], inc=stn)
                V(lambda E: E.memset(s_ar2[:], 0.0),
                  waits=[(stn, tA), (q_tiny, T_TINY)])
                v_a0 = V(lambda E: E.tensor_copy(s_ar2[0:16, :], p_A[0:16, 0:128]))
                t_a1 = dma("gpsimd", q_ar, s_ar2[32:48, :], s_ar2[0:16, :],
                           waits=[(sv, v_a0)])
                v_arin = V(lambda E: E.tensor_scalar(
                    s_arin[:], s_ar2[:], s_m32[:, 0:1], None, OP.mult),
                    waits=[(q_ar, t_a1)])
                tAR1 = dma("gpsimd", q_ar, arin_d, s_arin[:], waits=[(sv, v_arin)])
                cc_t = mk("gpsimd", lambda E: E.collective_compute(
                    "AllReduce", OP.add, replica_groups=replica_groups,
                    ins=[arin_d.opt()], outs=[arout_d.opt()]),
                    waits=[(q_ar, tAR1)], inc=scc)
                tAR2 = dma("gpsimd", q_ar, s_r0[:],
                           arout_d.rearrange("a b -> (a b)").rearrange("(o n) -> o n", o=1)[:, 0:D],
                           waits=[(scc, cc_t)])
                tAR3 = dma("gpsimd", q_ar, s_r1[:],
                           arout_d.rearrange("a b -> (a b)").rearrange("(o n) -> o n", o=1)[:, 2 * D:3 * D])
                v_sel = v_arin
                for (dst, srow, off) in ((s_posb, s_r0, 0), (s_kvt, s_r1, 0)):
                    for j in range(4):
                        t = Tn(lambda E, srow=srow, j=j: E.matmul(
                            p_A2[:, 0:512], s_onesf[:],
                            srow[0:1, 512 * j:512 * (j + 1)],
                            start=True, stop=True),
                            waits=[(q_ar, tAR3), (sv, v_sel)], inc=stn)
                        v_sel = V(lambda E, dst=dst, j=j: E.tensor_copy(
                            dst[:, 512 * j:512 * (j + 1)], p_A2[:, 0:512]),
                            waits=[(stn, t)])
                V(lambda E: E.tensor_scalar(
                    s_kvt[:, D:2 * D], s_posb[:], s_selp[:, 0:1], None, OP.mult),
                    waits=[(sv, v_sel), (q_tiny, T_TINY)])
                V(lambda E: E.scalar_tensor_tensor(
                    s_kvt[:, D:2 * D], s_kvt[:, 0:D], s_selp[:, 1:2],
                    s_kvt[:, D:2 * D], OP.mult, OP.add))
                V(lambda E: E.tensor_scalar(
                    s_kvt[:, D:2 * D], s_kvt[:, D:2 * D], 0.5, 0.5,
                    OP.mult, OP.add))
                V(lambda E: E.tensor_copy(s_gA[:, 0:1024:2],
                                          s_kvt[:, D:D + 512]))
                V(lambda E: E.tensor_copy(s_gA[:, 1:1024:2],
                                          s_kvt[:, D + 512:D + 1024]))
                V(lambda E: E.tensor_copy(s_gA[:, 1024:2048],
                                          s_kvt[:, D + 1024:D + 2048]))

                V(lambda E: E.tensor_scalar(s_nrsg[:], s_rsg[:], -128.0, None, OP.mult))
                v_nrs = V(lambda E: E.tensor_scalar(
                    s_nrsu[:], s_rsu[:], -128.0, None, OP.mult))
                t = Tn(lambda E: E.transpose(p_A[0:16, 0:128], s_rsd[:, 0:16], s_eyef[:]),
                       waits=[(sv, v_nrs)], inc=stn)
                V(lambda E: E.tensor_copy(s_prsdf[:], p_A[0:16, 0:128]), waits=[(stn, t)])
                V(lambda E: E.tensor_scalar(s_prsdf[:], s_prsdf[:], 0.5,
                                            None, OP.mult))
                v_prsd = V(lambda E: E.tensor_copy(s_prsdb[:], s_prsdf[:]))
                tP1 = dma("gpsimd", q_ar, prsb_d, s_prsdb[:], waits=[(sv, v_prsd)])

                barrier()
                flush_prog(block)

            with ExitStack() as PB:
                SB = lambda n, s, dt=F32: PB.enter_context(nc.sbuf_tensor(n, s, dt))
                PS = lambda n, s: PB.enter_context(nc.psum_tensor(n, s, F32))
                s_wgu = SB("s_wgu", [128, NOB * 512], BF16)
                s_jA = SB("s_jA", [128, 512])
                s_jB = SB("s_jB", [128, 512])
                s_prsrow = SB("s_prsrow", [1, D], BF16)
                s_xld = SB("s_xld", [128, D])
                s_xn = SB("s_xn", [128, D])
                s_x1 = SB("s_x1", [128, 4 * D])
                s_hz = SB("s_hz", [128, NOB * 512], BF16)
                s_xq = SB("s_xq", [128, D], BF16)
                s_xqT = SB("s_xqT", [128, NKD * 512], BF16)
                s_run = SB("s_run", [128, 512], BF16)
                s_habs = SB("s_habs", [128, 512], BF16)
                s_sil = SB("s_sil", [128, 512])
                s_sil2 = SB("s_sil2", [128, 512])
                s_is1bc = SB("s_is1bc", [128, 512])
                s_w3bc = SB("s_w3bc", [128, 512])
                s_oc = [SB(f"s_oc{i}", [128, 512]) for i in range(2)]
                s_s1c = SB("s_s1c", [128, 4])
                s_is1c = SB("s_is1c", [128, 4])
                s_ccol = SB("s_ccol", [128, 4])
                s_dcol = SB("s_dcol", [128, 4])
                s_w3col = SB("s_w3col", [128, 4])
                s_rowt = SB("s_rowt", [4, 128])
                s_flatt = SB("s_flatt", [1, 512])
                s_vcol = SB("s_vcol", [128, 1])
                s_rcol = SB("s_rcol", [128, 1])
                s_ncol = SB("s_ncol", [128, 1])
                s_acx = SB("s_acx", [128, 1])
                s_bn = SB("s_bn", [128, 24])
                s_bag = SB("s_bag", [128, 2])
                p_gu = [PS(f"p_gu{i}", [128, 512]) for i in range(4)]
                t_prsr = dma("gpsimd", q_ar, s_prsrow[:],
                             prsb_d.rearrange("a b -> (a b)").rearrange("(o n) -> o n", o=1),
                             waits=[(q_ar, tP1)])
                p_dn = [PS(f"p_dn{i}", [128, 512]) for i in range(2)]
                p_x = PB.enter_context(nc.psum_tensor("p_x", [128, 1024], BF16))
                p_c = PS("p_c", [128, 512])

                NSLOT = 5
                slot_ld = [None] * NSLOT
                dn_evac = [0, 0, 0, 0]
                dnA_done = 0
                dnB_done = 0
                x_free = 0
                px_free = 0
                gu_free = [0, 0]
                dn_free = [0, 0]
                oc_free = [0, 0]
                w_last_mm = 0

                for st in range(NST):
                    st0 = ST * st
                    xq_last = 0
                    for t4 in range(4):
                        tok0 = st0 + 128 * t4
                        wx = [(sv, x_free)] if x_free else []
                        tX = dma("sync", q_x, s_xld[:], x_d[tok0:tok0 + 128, :],
                                 waits=wx)
                        for q in range(4):
                            V(lambda E, q=q: E.bn_stats(
                                s_bn[:, 6 * q:6 * (q + 1)],
                                s_xld[:, 512 * q:512 * (q + 1)]),
                                waits=[(q_x, tX)] if q == 0 else [])
                        V(lambda E: E.bn_aggr(s_bag[:], s_bn[:]))
                        v_ve = V(lambda E: E.tensor_scalar(
                            s_vcol[:], s_bag[:, 1:2], LN_EPS, None, OP.add))
                        s = Sc(lambda E: E.activation(s_vcol[:], s_vcol[:], AF.Sqrt),
                               waits=[(sv, v_ve)])
                        V(lambda E: E.reciprocal(s_rcol[:], s_vcol[:]),
                          waits=[(ss, s)])
                        v_ln = V(lambda E: E.scalar_tensor_tensor(
                            s_ncol[:], s_bag[:, 0:1], -1.0, s_rcol[:],
                            OP.mult, OP.mult))
                        s = Sc(lambda E: E.activation(
                            s_xn[:], s_xld[:], AF.Identity, bias=s_ncol[:, 0:1],
                            scale=s_rcol[:, 0:1]), waits=[(sv, v_ln)])
                        V(lambda E: E.tensor_tensor(s_xn[:], s_xn[:], s_gA[:],
                                                    OP.mult), waits=[(ss, s)])
                        x_free = V(lambda E, t4=t4: E.tensor_tensor(
                            s_x1[:, D * t4:D * (t4 + 1)], s_xn[:], s_xld[:], OP.add))
                        for q in range(4):
                            V(lambda E, q=q, t4=t4: E.bn_stats(
                                s_bn[:, 6 * q:6 * (q + 1)],
                                s_x1[:, D * t4 + 512 * q:D * t4 + 512 * (q + 1)]))
                        V(lambda E: E.bn_aggr(s_bag[:], s_bn[:]))
                        v_ve = V(lambda E: E.tensor_scalar(
                            s_vcol[:], s_bag[:, 1:2], LN_EPS, None, OP.add))
                        s = Sc(lambda E: E.activation(s_vcol[:], s_vcol[:], AF.Sqrt),
                               waits=[(sv, v_ve)])
                        V(lambda E: E.reciprocal(s_rcol[:], s_vcol[:]),
                          waits=[(ss, s)])
                        v_ln2 = V(lambda E: E.scalar_tensor_tensor(
                            s_ncol[:], s_bag[:, 0:1], -1.0, s_rcol[:],
                            OP.mult, OP.mult))
                        s = Sc(lambda E, t4=t4: E.activation(
                            s_xn[:], s_x1[:, D * t4:D * (t4 + 1)], AF.Identity,
                            bias=s_ncol[:, 0:1], scale=s_rcol[:, 0:1]),
                            waits=[(sv, v_ln2)])
                        V(lambda E: E.reduce_max(
                            s_acx[:], s_xn[:], axis=AX, apply_absolute_value=True),
                            waits=[(ss, s)])
                        V(lambda E: E.tensor_scalar(
                            s_acx[:], s_acx[:], Q_EPS, None, OP.max))
                        V(lambda E, t4=t4: E.tensor_scalar(
                            s_is1c[:, t4:t4 + 1], s_acx[:], 1.0 / 127.0, None,
                            OP.mult))
                        V(lambda E: E.reciprocal(s_rcol[:], s_acx[:]))
                        V(lambda E, t4=t4: E.tensor_scalar(
                            s_s1c[:, t4:t4 + 1], s_rcol[:], 127.0, None, OP.mult))
                        v_xq = 0
                        for q in range(4):
                            sl = slice(512 * q, 512 * (q + 1))
                            wq_ = [(ss, px_free)] if (q == 0 and px_free) else []
                            V(lambda E, sl=sl, t4=t4: E.tensor_scalar(
                                s_jA[:], s_xn[:, sl], s_s1c[:, t4:t4 + 1], 384.5,
                                OP.mult, OP.add), waits=wq_)
                            V(lambda E: E.tensor_scalar(
                                s_jA[:].bitcast(I32), s_jA[:].bitcast(I32),
                                MASK15, None, OP.bitwise_and))
                            v_xq = V(lambda E, sl=sl: E.tensor_scalar(
                                s_xq[:, sl], s_jA[:], -256.0, None, OP.add))
                        for nb in range(4):
                            wtx = [(sv, v_xq)] if nb == 0 else []
                            if px_free:
                                wtx = wtx + [(ss, px_free)]
                            t = 0
                            for j in range(4):
                                k = 4 * nb + j
                                t = Tn(lambda E, j=j, k=k: E.transpose(
                                    p_x[:, 128 * j:128 * (j + 1)],
                                    s_xq[:, 128 * k:128 * (k + 1)], s_eyeb[:]),
                                    waits=wtx if j == 0 else [],
                                    inc=stn if j == 3 else None)
                            px_free = Sc(lambda E, nb=nb, t4=t4: E.copy(
                                s_xqT[:].rearrange("p (k c) -> p k c", c=512)
                                [:, 4 * nb:4 * nb + 4, 128 * t4:128 * (t4 + 1)],
                                p_x[:, 0:512].rearrange("p (j c) -> p j c", c=128)),
                                waits=[(stn, t)])
                        xq_last = px_free
                    t = Tn(lambda E: E.transpose(p_c[0:4, 0:128], s_is1c[:],
                                                 s_eyef[:]),
                           waits=[(sv, sv.n)], inc=stn)
                    v_r2 = V(lambda E: E.tensor_copy(s_rowt[:], p_c[0:4, 0:128]),
                             waits=[(stn, t)])
                    tB1 = dma("gpsimd", q_ar, rowf_d, s_rowt[:], waits=[(sv, v_r2)])
                    tB2 = dma("gpsimd", q_ar, s_flatt[:],
                              rowf_d.rearrange("a b -> (a b)").rearrange(
                                  "(o n) -> o n", o=1), waits=[(q_ar, tB1)])
                    t = Tn(lambda E: E.matmul(
                        p_c[:, 0:512], s_onesf[:], s_flatt[:],
                        start=True, stop=True),
                        waits=[(q_ar, tB2), (sv, v_r2)], inc=stn)
                    v_bc = V(lambda E: E.tensor_copy(s_is1bc[:], p_c[:, 0:512]),
                             waits=[(stn, t)])

                    run_last = 0

                    def load_gu(ob, guard):
                        sl = ob % NSLOT
                        b0 = 4096 * sl
                        qg = q_wl[ob % 2]
                        qu = qc if ob % 2 == 0 else q_u2
                        return (
                            qg, dma("sync", qg, s_wgu[:, b0:b0 + 2048],
                                    wqgt_d[ob], waits=list(guard) + [(qg, qg.n)]),
                            qu, dma("scalar", qu,
                                    s_wgu[:, b0 + 2048:b0 + 4096],
                                    wqut_d[ob], waits=list(guard) + [(qu, qu.n)]))

                    for ob in range(NSLOT):
                        guard = []
                        if ob == 0:
                            guard = ([(stn, w_last_mm)] if w_last_mm else
                                     [(q, q.n) for q in (q_st[0], q_st[1])])
                        slot_ld[ob] = load_gu(ob, guard)
                    for ob in range(NOB):
                        sl = ob % NSLOT
                        b0 = 4096 * sl
                        qg_, tG_, qu_, tU_ = slot_ld[sl]
                        pgb = p_gu[2 * (ob % 2)]
                        pub = p_gu[2 * (ob % 2) + 1]
                        bw = [(qg_, tG_), (qu_, tU_), (ss, xq_last),
                              (sv, v_bc)]
                        if gu_free[ob % 2]:
                            bw.append((sv, gu_free[ob % 2]))
                        for k in range(NKD):
                            Tn(lambda E, k=k, b0=b0, pgb=pgb: E.matmul(
                                pgb[:, 0:512],
                                s_wgu[:, b0 + 128 * k:b0 + 128 * (k + 1)],
                                s_xqT[:, 512 * k:512 * (k + 1)],
                                start=(k == 0), stop=(k == NKD - 1)),
                                waits=bw if k == 0 else [])
                        for k in range(NKD):
                            w_last_mm = Tn(
                                lambda E, k=k, b0=b0, pub=pub: E.matmul(
                                    pub[:, 0:512],
                                    s_wgu[:, b0 + 2048 + 128 * k:
                                          b0 + 2048 + 128 * (k + 1)],
                                    s_xqT[:, 512 * k:512 * (k + 1)],
                                    start=(k == 0), stop=(k == NKD - 1)),
                                inc=stn if k == NKD - 1 else None)
                        t_gu = w_last_mm
                        if ob + NSLOT < NOB:
                            slot_ld[sl] = load_gu(ob + NSLOT, [(stn, t_gu)])
                        v_si = V(lambda E, ob=ob, pgb=pgb: E.scalar_tensor_tensor(
                            s_sil[:], pgb[:, 0:512], s_nrsg[:, ob:ob + 1],
                            s_is1bc[:], OP.add, OP.mult), waits=[(stn, t_gu)])
                        s = Sc(lambda E: E.activation(s_sil2[:], s_sil[:],
                                                      AF.Silu),
                               waits=[(sv, v_si)])
                        v = V(lambda E, ob=ob, pub=pub: E.scalar_tensor_tensor(
                            s_hz[:, 512 * ob:512 * (ob + 1)], pub[:, 0:512],
                            s_nrsu[:, ob:ob + 1], s_sil2[:], OP.add, OP.mult),
                            waits=[(ss, s)])
                        gu_free[ob % 2] = v
                        abw = [(sv, v)]
                        if run_last:
                            abw.append((sv, run_last))
                        s_ab = Sc(lambda E, ob=ob: E.activation(
                            s_habs[:], s_hz[:, 512 * ob:512 * (ob + 1)],
                            AF.Abs), waits=abw)
                        if ob == 0:
                            run_last = V(lambda E: E.tensor_copy(
                                s_run[:], s_habs[:]), waits=[(ss, s_ab)])
                        else:
                            run_last = V(lambda E: E.tensor_tensor(
                                s_run[:], s_run[:], s_habs[:], OP.max),
                                waits=[(ss, s_ab)])
                    t_am = 0
                    for tau in range(4):
                        wam = [(sv, run_last), (ss, px_free)] if tau == 0 else []
                        t_am = Tn(lambda E, tau=tau: E.transpose(
                            p_x[:, 128 * tau:128 * (tau + 1)],
                            s_run[:, 128 * tau:128 * (tau + 1)], s_eyeb[:]),
                            waits=wam, inc=stn if tau == 3 else None)
                    px_free = Sc(lambda E: E.copy(s_habs[:], p_x[:, 0:512]),
                                 waits=[(stn, t_am)])
                    for tau in range(4):
                        V(lambda E, tau=tau: E.reduce_max(
                            s_dcol[:, tau:tau + 1],
                            s_habs[:, 128 * tau:128 * (tau + 1)], axis=AX),
                            waits=[(ss, px_free)] if tau == 0 else ())
                    for t4 in range(4):
                        V(lambda E, t4=t4: E.tensor_scalar(
                            s_vcol[:], s_s1c[:, t4:t4 + 1], Q_EPS, None, OP.mult))
                        V(lambda E, t4=t4: E.tensor_tensor(
                            s_dcol[:, t4:t4 + 1], s_dcol[:, t4:t4 + 1],
                            s_vcol[:], OP.max))
                        V(lambda E, t4=t4: E.reciprocal(
                            s_rcol[:], s_dcol[:, t4:t4 + 1]))
                        V(lambda E, t4=t4: E.tensor_scalar(
                            s_w3col[:, t4:t4 + 1], s_rcol[:], 127.0, None,
                            OP.mult))
                        V(lambda E, t4=t4: E.scalar_tensor_tensor(
                            s_ccol[:, t4:t4 + 1], s_dcol[:, t4:t4 + 1],
                            1.0 / 127.0, s_is1c[:, t4:t4 + 1],
                            OP.mult, OP.mult))
                    t = Tn(lambda E: E.transpose(p_c[0:4, 0:128], s_w3col[:],
                                                 s_eyef[:]),
                           waits=[(sv, sv.n)], inc=stn)
                    v = V(lambda E: E.tensor_copy(s_rowt[:], p_c[0:4, 0:128]),
                          waits=[(stn, t)])
                    tB3 = dma("gpsimd", q_ar, rowf_d, s_rowt[:], waits=[(sv, v)])
                    tB4 = dma("gpsimd", q_ar, s_flatt[:],
                              rowf_d.rearrange("a b -> (a b)").rearrange(
                                  "(o n) -> o n", o=1), waits=[(q_ar, tB3)])
                    t = Tn(lambda E: E.matmul(
                        p_c[:, 0:512], s_onesf[:], s_flatt[:],
                        start=True, stop=True),
                        waits=[(q_ar, tB4), (sv, v)], inc=stn)
                    v_w3 = V(lambda E: E.tensor_copy(s_w3bc[:], p_c[:, 0:512]),
                             waits=[(stn, t)])
                    z_last = 0
                    for ob0 in range(0, NOB, 2):
                        obs = [(ob0, s_jA)] + ([(ob0 + 1, s_jB)]
                                               if ob0 + 1 < NOB else [])
                        for i, (ob, jt) in enumerate(obs):
                            V(lambda E, ob=ob, jt=jt: E.tensor_tensor(
                                jt[:], s_hz[:, 512 * ob:512 * (ob + 1)],
                                s_w3bc[:], OP.mult),
                                waits=[(sv, v_w3)] if ob == 0 else [],
                                dr=(i == len(obs) - 1))
                        for i, (ob, jt) in enumerate(obs):
                            V(lambda E, jt=jt: E.tensor_scalar(
                                jt[:], jt[:], 384.5, None, OP.add),
                                dr=(i == len(obs) - 1))
                        for i, (ob, jt) in enumerate(obs):
                            V(lambda E, jt=jt: E.tensor_scalar(
                                jt[:].bitcast(I32), jt[:].bitcast(I32),
                                MASK15, None, OP.bitwise_and),
                                dr=(i == len(obs) - 1))
                        for i, (ob, jt) in enumerate(obs):
                            z_last = V(lambda E, ob=ob, jt=jt: E.tensor_scalar(
                                s_hz[:, 512 * ob:512 * (ob + 1)], jt[:], -384.5,
                                None, OP.add), dr=(i == len(obs) - 1))
                    FHA = 22
                    for dc in range(4):
                        d0 = 512 * dc
                        wA = ([(stn, dnA_done)] if dnA_done else
                              [(stn, w_last_mm)])
                        tWDA = dma("gpsimd", q_st[0],
                                   s_wgu[:, 0:FHA * 512].rearrange(
                                       "p (f j) -> p f j", j=512),
                                   wqdt_d[0:128 * FHA, d0:d0 + 512].rearrange(
                                       "(f p) j -> p f j", p=128),
                                   waits=wA + [(q_st[0], q_st[0].n)])
                        wB = ([(stn, dnB_done)] if dnB_done else
                              [(stn, w_last_mm)])
                        tWDB = dma("gpsimd", q_st[1],
                                   s_wgu[:, FHA * 512:NOB * 512].rearrange(
                                       "p (f j) -> p f j", j=512),
                                   wqdt_d[128 * FHA:, d0:d0 + 512].rearrange(
                                       "(f p) j -> p f j", p=128),
                                   waits=wB + [(q_st[1], q_st[1].n)])
                        for t4 in range(4):
                            bw = [(q_st[0], tWDA), (sv, z_last)]
                            if dn_evac[t4]:
                                bw.append((sv, dn_evac[t4]))
                            for f in range(FHA):
                                dnA_done = Tn(
                                    lambda E, f=f, t4=t4: E.matmul(
                                        p_gu[t4][:, 0:512],
                                        s_hz[:, 512 * f + 128 * t4:
                                             512 * f + 128 * (t4 + 1)],
                                        s_wgu[:, 512 * f:512 * (f + 1)],
                                        start=(f == 0), stop=False),
                                    waits=bw if f == 0 else [],
                                    inc=stn if (t4 == 3 and f == FHA - 1)
                                    else None)
                        t_dns = []
                        for t4 in range(4):
                            for f in range(FHA, NOB):
                                Tn(lambda E, f=f, t4=t4: E.matmul(
                                    p_gu[t4][:, 0:512],
                                    s_hz[:, 512 * f + 128 * t4:
                                         512 * f + 128 * (t4 + 1)],
                                    s_wgu[:, 512 * f:512 * (f + 1)],
                                    start=False, stop=False),
                                    waits=[(q_st[1], tWDB)]
                                    if (t4 == 0 and f == FHA) else [])
                            t_dn = 0
                            for q in range(4):
                                t_dn = Tn(lambda E, q=q, dc=dc, t4=t4: E.matmul(
                                    p_gu[t4][:, 128 * q:128 * (q + 1)],
                                    s_ones_row_b[:],
                                    s_prsrow[0:1, 512 * dc + 128 * q:
                                             512 * dc + 128 * (q + 1)],
                                    start=False, stop=(q == 3)),
                                    inc=stn if q == 3 else None)
                            t_dns.append(t_dn)
                        dnB_done = t_dns[-1]
                        w_last_mm = t_dns[-1]
                        for t4 in range(4):
                            tok0 = st0 + 128 * t4
                            par = (dc * 4 + t4) % 2
                            wv = [(stn, t_dns[t4])]
                            if oc_free[par]:
                                wv.append((q_y[par], oc_free[par]))
                            v = V(lambda E, t4=t4, dc=dc, par=par:
                                  E.scalar_tensor_tensor(
                                      s_oc[par][:], p_gu[t4][:, 0:512],
                                      s_ccol[:, t4:t4 + 1],
                                      s_x1[:, D * t4 + 512 * dc:
                                           D * t4 + 512 * (dc + 1)],
                                      OP.mult, OP.add), waits=wv)
                            dn_evac[t4] = v
                            oc_free[par] = dma(
                                "gpsimd", q_y[par],
                                y_d[tok0:tok0 + 128, d0:d0 + 512],
                                s_oc[par][:], waits=[(sv, v)])
                    barrier()
                barrier()
                flush_prog(block)
    return nc



def kernel(L0_attn_X, L0_attn_W, L0_attn_b, L0_ffn_w_gate, L0_ffn_w_up,
           L0_ffn_w_down):
    X = np.asarray(L0_attn_X)
    W = np.asarray(L0_attn_W)
    b = np.asarray(L0_attn_b)
    wg = np.ascontiguousarray(np.asarray(L0_ffn_w_gate), np.float32)
    wu = np.ascontiguousarray(np.asarray(L0_ffn_w_up), np.float32)
    wd = np.ascontiguousarray(np.asarray(L0_ffn_w_down), np.float32)

    perm = np.concatenate([np.arange(0, 1024, 2), np.arange(1, 1024, 2),
                           np.arange(1024, 2048)])
    fperm = np.concatenate([np.concatenate([perm[128 * c:128 * (c + 1)],
                                            2048 + perm[128 * c:128 * (c + 1)]])
                            for c in range(16)])
    Wp = np.ascontiguousarray(np.asarray(W, np.float32)[:, fperm])
    bp = np.ascontiguousarray(np.asarray(b, np.float32)[fperm].reshape(1, F))
    eyeb = np.eye(128, dtype=ml_dtypes.bfloat16)
    eyef = np.eye(128, dtype=np.float32)
    jc = (np.arange(4)[None, :] * 128
          + np.arange(128)[:, None]).astype(np.float32)

    nc = build_nc()
    in_maps = []
    for c in range(NCORES):
        bc, sc = c // 4, c % 4
        xs = np.ascontiguousarray(X[bc, T * sc:T * (sc + 1), :], dtype=np.float32)
        m32 = np.zeros((64, 1), np.float32)
        m32[32 * bc:32 * bc + 16] = 1.0
        in_maps.append(dict(
            xt=np.ascontiguousarray(xs.T), x=xs, wkv=Wp, bkv=bp,
            wg=wg, wu=wu, wd=wd,
            pos=np.arange(T * sc, T * (sc + 1), dtype=np.float32).reshape(1, T),
            jc=jc, eyeb=eyeb, eyef=eyef,
            selp=np.tile(np.array([[1.0 - bc, float(bc)]], np.float32), (128, 1)),
            m32=m32,
        ))
    res = run_bass_kernel_spmd(nc, in_maps, core_ids=list(range(NCORES)))
    out = np.empty((2, 8192, 2048), np.float32)
    for c in range(NCORES):
        bc, sc = c // 4, c % 4
        out[bc, T * sc:T * (sc + 1), :] = res.results[c]["y"]
    return out
